# version 06
# baseline (speedup 1.0000x reference)
import numpy as np
import ml_dtypes
import concourse.bass as bass
import concourse.mybir as mybir
from concourse.bass_utils import run_bass_kernel_spmd

F32 = mybir.dt.float32
BF16 = mybir.dt.bfloat16
AF = mybir.ActivationFunctionType
ALU = mybir.AluOpType
AX = mybir.AxisListType

T = 2048
C = 1024
HID = 4096
NH = 16
DH = 64
LAT = 256
QG = 4
CB = C // 128
IBH = HID // 128
LN_EPS = 1e-5
SCALE_A = 0.125
SCALE_B = 0.03125
N_SUB = 4
KINDSEQ = ["attn_a", "ffn", "attn_b", "ffn"]
DBG_NO_BIAS = False
DBG_RAW = False

ENGINES = ("sync", "gpsimd", "tensor", "vector", "scalar")

BASE = 16512
OFF_XN = BASE + 0
OFF_QTS = BASE + 32768
OFF_LAT = BASE + 40960
OFF_AA = BASE + 0
OFF_KT = BASE + 65536
OFF_VP = BASE + 98304
OFF_GG = BASE + 65536
OFF_AO = BASE + 131584
OFF_WR = BASE + 147968
OFF_XS = BASE + 172544
OFF_BALL = BASE + 178688
OFF_PT = BASE + 194560
OFF_MASK = BASE + 198656
OFF_RECB = BASE + 200448
OFF_ROWS = BASE + 202496
OFF_CONST = BASE + 208640
SBUF_END = 229376


def ceil32(x):
    return (x + 31) // 32 * 32


class Sched:
    def __init__(self):
        self.q = {e: [] for e in ENGINES}
        self.c = {}

    def cur(self, sem):
        return (sem, self.c.get(sem, 0))

    def add(self, eng, fn, waits=(), inc=None, amt=1):
        tok = None
        for (s, v) in waits:
            assert v <= self.c.get(s, 0), f"wait on future token {s}={v}"
        if inc is not None:
            self.c[inc] = self.c.get(inc, 0) + amt
            tok = (inc, self.c[inc])
        self.q[eng].append((fn, tuple(waits), inc, amt))
        return tok

    def dma(self, eng, out, in_, waits=(), sem=None, **kw):
        def fn(e, out=out, in_=in_, kw=kw):
            return e.dma_start(out=out, in_=in_, **kw)
        return self.add(eng, fn, waits=waits, inc=sem, amt=16)

    def emit(self, block, sems):
        for name in ENGINES:
            ops = self.q[name]

            def make(ops):
                def body(eng):
                    waited = {}
                    for fn, waits, inc, amt in ops:
                        for (s, v) in waits:
                            if v > 0 and waited.get(s, 0) < v:
                                eng.wait_ge(sems[s], v)
                                waited[s] = v
                        if fn is None:
                            continue
                        inst = fn(eng)
                        if inc is not None:
                            inst.then_inc(sems[inc], amt)
                return body
            getattr(block, name)(make(ops))


def build_nc(n_sub=None):
    if n_sub is None:
        n_sub = N_SUB
    nc = bass.Bass("TRN2", target_bir_lowering=False)

    def din(name, shape):
        return nc.dram_tensor(name, shape, F32, kind="ExternalInput").ap()

    x_in = din("x", [C, T])
    dbgd = nc.dram_tensor("dbg", [NH, T], F32, kind="ExternalOutput").ap()
    y_out = nc.dram_tensor("y", [C, T], F32, kind="ExternalOutput").ap()
    xres = nc.dram_tensor("xres", [C, T], F32, kind="Internal").ap()
    extd = nc.dram_tensor("extd", [16, 4095], F32, kind="Internal").ap()

    aa_wq_lat = din("aa_wq_lat", [C, LAT])
    aa_wk_lat = din("aa_wk_lat", [C, LAT])
    aa_wv_lat = din("aa_wv_lat", [C, LAT])
    aa_wq_h = din("aa_wq_h", [LAT, C])
    aa_wk_h = din("aa_wk_h", [LAT, C])
    aa_wv_h = din("aa_wv_h", [LAT, C])
    aa_wproj = din("aa_wproj", [C, C])
    aa_bproj = din("aa_bproj", [128, CB])
    ab_wqT = din("ab_wqT", [2 * 128, 8 * 512])
    ab_wkT = din("ab_wkT", [2 * 128, 8 * 512])
    ab_wvT = din("ab_wvT", [2 * 128, 8 * 512])
    ab_tab = din("ab_tab", [2047, NH])
    ab_gamma = din("ab_gamma", [128, CB])
    ab_beta = din("ab_beta", [128, CB])
    f_w1T = din("f_w1T", [32 * 128, 8 * 128])
    f_w2T = din("f_w2T", [32 * 128, 8 * 128])
    f_h1T = din("f_h1T", [32 * 128, 32 * 128])
    f_h2T = din("f_h2T", [32 * 128, 32 * 128])
    f_w3T = din("f_w3T", [8 * 128, 32 * 128])
    f_s1 = din("f_s1", [128, IBH])
    f_s2 = din("f_s2", [128, IBH])
    f_h1s = din("f_h1s", [128, IBH])
    f_h2s = din("f_h2s", [128, IBH])
    f_w3s = din("f_w3s", [128, CB])
    c_maskW = din("c_maskW", [128, 896])
    c_ones_row = din("c_ones_row", [1, 128])
    c_ones_col = din("c_ones_col", [128, 1])

    def sat(name, shape, dt, off):
        return nc.alloc_sbuf_tensor_at(name, shape, dt, offset=off).ap()

    xn = sat("xn", [128, CB * T], BF16, OFF_XN)
    qts = sat("qts", [128, CB * 512], BF16, OFF_QTS)
    lat_q = sat("lat_q", [128, 2 * T], BF16, OFF_LAT)
    lat_k = sat("lat_k", [128, 2 * T], BF16, OFF_LAT + 8192)
    lat_v = sat("lat_v", [128, 2 * T], BF16, OFF_LAT + 16384)
    AAt = sat("AAt", [128, IBH * 1024], BF16, OFF_AA)
    KT = sat("KT", [128, CB * T], BF16, OFF_KT)
    Vp = sat("Vp", [128, 16 * 1040], BF16, OFF_VP)
    GG = sat("GG", [128, IBH * 1024], BF16, OFF_GG)
    ext_sb = sat("ext_sb", [16, 4095], F32, OFF_AA)
    ao_f = sat("ao_f", [128, CB * 512], F32, OFF_AO)
    ao_bf = sat("ao_bf", [128, CB * 512], BF16, OFF_AO)
    xnc = sat("xnc", [128, CB * 1024], BF16, OFF_AO)
    wring = [sat(f"wr{i}", [128, 4096], BF16, OFF_WR + i * 8192) for i in range(3)]
    xsr = [sat(f"xs{i}", [128, 512], F32, OFF_XS + i * 2048) for i in range(2)]
    ballb = [sat(f"ball{i}", [128, 3968], BF16, OFF_BALL + i * 7936) for i in range(2)]
    ptr = sat("ptr", [128, 4 * 512], BF16, OFF_PT)
    silu_sb = sat("silu_sb", [128, 1024], F32, OFF_PT)
    maskW = sat("maskW", [128, 896], BF16, OFF_MASK)
    recipB = sat("recipB", [64, 512], F32, OFF_RECB)
    rows = sat("rows", [1, 1536], F32, OFF_ROWS)
    co = OFF_CONST
    ones_row = sat("ones_row", [1, 128], F32, co); co = ceil32(co + 512)
    ones_col = sat("ones_col", [128, 1], BF16, co); co = ceil32(co + 4)
    ones_col_f = sat("ones_col_f", [128, 1], F32, co); co = ceil32(co + 4)
    s1c = sat("s1c", [128, IBH], F32, co); co = ceil32(co + IBH * 4)
    s2c = sat("s2c", [128, IBH], F32, co); co = ceil32(co + IBH * 4)
    h1sc = sat("h1sc", [128, IBH], F32, co); co = ceil32(co + IBH * 4)
    h2sc = sat("h2sc", [128, IBH], F32, co); co = ceil32(co + IBH * 4)
    w3sc = sat("w3sc", [128, CB], F32, co); co = ceil32(co + CB * 4)
    bprojc = sat("bprojc", [128, CB], F32, co); co = ceil32(co + CB * 4)
    gammac = sat("gammac", [128, CB], F32, co); co = ceil32(co + CB * 4)
    betac = sat("betac", [128, CB], F32, co); co = ceil32(co + CB * 4)
    biascol = sat("biascol", [128, 1], F32, co); co = ceil32(co + 4)
    kmaxc = sat("kmaxc", [128, 1], F32, co); co = ceil32(co + 4)
    qmaxc = sat("qmaxc", [128, 1], F32, co); co = ceil32(co + 4)
    scr_row = sat("scr_row", [1, 128], F32, co); co = ceil32(co + 512)
    scal = sat("scal", [1, 4], F32, co); co = ceil32(co + 16)
    tsub = sat("tsub", [128, 512], F32, co); co = ceil32(co + 2048)
    t2sb = sat("t2sb", [128, 512], F32, OFF_XS + 4096)
    assert co <= SBUF_END, f"sbuf overflow {co}"

    pS = [nc.alloc_psum_tensor(f"pS{i}", [128, 1024], F32).ap() for i in range(4)]

    semnames = (("ld", "wd", "bd", "st", "pe", "act", "dve", "bl0", "bl1", "rb",
                 "xl0", "xl1") + tuple(f"wd{i}" for i in range(3))
                + tuple(f"wa{i}" for i in range(8)))
    from contextlib import ExitStack
    stack = ExitStack()
    sems = {s: stack.enter_context(nc.semaphore(s)) for s in semnames}

    S = Sched()
    r_mean = rows[0:1, 0:512]
    r_rstd = rows[0:1, 512:1024]
    r_tmp = rows[0:1, 1024:1536]
    r_recip = rows[0:1, 0:512]

    psfree = {}

    def ps_waits(i, part):
        out = []
        if part == "full":
            keys = [(i, 0), (i, 512), (i, "full")]
        else:
            keys = [(i, part), (i, "full")]
        for k in keys:
            if psfree.get(k) is not None:
                out.append(psfree[k])
        return out

    def ps_set(i, part, tok):
        psfree[(i, part)] = tok

    def mm(out, lhsT, rhs, start, stop, waits=(), inc=None):
        def fn(e, out=out, lhsT=lhsT, rhs=rhs, start=start, stop=stop):
            return e.matmul(out, lhsT, rhs, start=start, stop=stop)
        return S.add("tensor", fn, waits=waits, inc=inc)

    def act(fn_type, out, in_, waits=(), scale=1.0, bias=0.0):
        def fn(e, out=out, in_=in_, fn_type=fn_type, scale=scale, bias=bias):
            return e.activation(out, in_, fn_type, bias=bias, scale=scale)
        return S.add("scalar", fn, waits=waits, inc="act")

    def dve(method, *args, waits=(), **kw):
        def fn(e, method=method, args=args, kw=kw):
            return getattr(e, method)(*args, **kw)
        return S.add("vector", fn, waits=waits, inc="dve")

    def wload(dst, src, wait_toks, col0=0, ncol=None, sem="wd"):
        if ncol is None:
            ncol = src.shape[1]
        src_r = src.rearrange("(ib p) o -> p ib o", p=128)[:, :, col0:col0 + ncol]
        return S.dma("gpsimd", dst, src_r, waits=[w for w in wait_toks if w], sem=sem)

    S.dma("sync", ones_row[:, :], c_ones_row[:, :], sem="ld")
    S.dma("sync", s1c[:, :], f_s1[:, :], sem="ld")
    S.dma("sync", s2c[:, :], f_s2[:, :], sem="ld")
    S.dma("sync", h1sc[:, :], f_h1s[:, :], sem="ld")
    S.dma("sync", h2sc[:, :], f_h2s[:, :], sem="ld")
    S.dma("sync", w3sc[:, :], f_w3s[:, :], sem="ld")
    S.dma("sync", bprojc[:, :], aa_bproj[:, :], sem="ld")
    S.dma("sync", gammac[:, :], ab_gamma[:, :], sem="ld")
    tok_cst = S.dma("sync", betac[:, :], ab_beta[:, :], sem="ld")
    S.dma("gpsimd", ones_col[:, :], c_ones_col[:, :], sem="bd")
    S.dma("sync", ones_col_f[:, :], c_ones_col[:, :], sem="ld")
    tok_msk = S.dma("gpsimd", maskW[:, :], c_maskW[:, :], sem="bd")
    tok_tab = S.dma("sync", ext_sb[:, 0:2047], ab_tab.rearrange("a b -> b a"),
                    sem="ld", allow_slow_non_contiguous=True)
    tok_z = dve("memset", ext_sb[:, 2047:4095], 0.0, waits=[tok_tab])
    tok_fill = dve("tensor_scalar", ext_sb[:, 2047:4095], ext_sb[:, 2047:4095],
                   ext_sb[:, 2046:2047], None, ALU.add, waits=[tok_z])
    tok_ext = S.dma("sync", extd[:, :], ext_sb[:, :], waits=[tok_fill], sem="ld")
    setup_tok = (tok_cst, tok_msk, tok_ext)

    def emit_ln(src_dram, dst, dst_w, t_base, n_tok, after_toks=()):
        last_sq = None
        last_out = None
        for g in range(n_tok // 512):
            t0 = t_base + g * 512
            sum_ps = pS[2][0:1, 0:512]
            sumsq_ps = pS[2][0:1, 512:1024]
            mm_sum = mm_sq = None
            for cb in range(CB):
                sl = xsr[cb % 2]
                wt = list(after_toks) if (g == 0 and cb < 2) else []
                if last_out is not None:
                    wt.append(last_out)
                if mm_sum is not None and cb >= 2:
                    wt.append(mm_sum)
                if last_sq is not None:
                    wt.append(last_sq)
                ltok = S.dma("sync", sl[:, :],
                             src_dram[cb * 128:(cb + 1) * 128, t0:t0 + 512],
                             waits=wt, sem=f"xl{cb % 2}")
                sq = silu_sb[:, (cb % 2) * 512:(cb % 2) * 512 + 512]
                wsq = [ltok]
                if mm_sq is not None and cb >= 2:
                    wsq.append(mm_sq)
                sqtok = act(AF.Square, sq, sl[:, :], waits=wsq)
                w0 = ps_waits(2, 0) if cb == 0 else []
                mm_sum = mm(sum_ps, ones_col_f[:, :], sl[:, :], start=(cb == 0),
                            stop=(cb == CB - 1), waits=[ltok] + w0, inc="pe")
                w0 = ps_waits(2, 512) if cb == 0 else []
                mm_sq = mm(sumsq_ps, ones_col_f[:, :], sq, start=(cb == 0),
                           stop=(cb == CB - 1), waits=[sqtok] + w0, inc="pe")
                last_sq = sqtok
            tmean = dve("tensor_scalar", r_mean, sum_ps, 1.0 / C, None, ALU.mult,
                        waits=[mm_sum, mm_sq])
            tmsq = dve("tensor_tensor", r_tmp, r_mean, r_mean, ALU.mult, waits=[tmean])
            tvar = dve("scalar_tensor_tensor", r_rstd, sumsq_ps, 1.0 / C, r_tmp,
                       ALU.mult, ALU.subtract, waits=[tmsq])
            ps_set(2, 0, tvar)
            ps_set(2, 512, tvar)
            tvar = dve("tensor_scalar", r_rstd, r_rstd, LN_EPS, None, ALU.add,
                       waits=[tvar])
            tln = act(AF.Ln, r_tmp, r_rstd, waits=[tvar])
            tno = dve("tensor_scalar", r_tmp, r_tmp, -0.5, None, ALU.mult, waits=[tln])
            texp = act(AF.Exp, r_rstd, r_tmp, waits=[tno])
            meanB = pS[3][:, 0:512]
            rstdB = pS[3][:, 512:1024]
            tbm = mm(meanB, ones_row[:, :], r_mean, start=True, stop=True,
                     waits=[tmsq] + ps_waits(3, 0), inc="pe")
            tbr = mm(rstdB, ones_row[:, :], r_rstd, start=True, stop=True,
                     waits=[texp] + ps_waits(3, 512), inc="pe")
            for cb in range(CB):
                sl = xsr[cb % 2]
                wt = [last_out] if last_out is not None else []
                ltok = S.dma("sync", sl[:, :],
                             src_dram[cb * 128:(cb + 1) * 128, t0:t0 + 512],
                             waits=wt, sem=f"xl{cb % 2}")
                tsu = dve("tensor_tensor", tsub[:, :], sl[:, :], meanB, ALU.subtract,
                          waits=[ltok, tbm, tbr])
                dst_sl = dst[:, cb * dst_w + (t0 - t_base): cb * dst_w + (t0 - t_base) + 512]
                last_out = dve("tensor_tensor", dst_sl, tsub[:, :], rstdB, ALU.mult,
                               waits=[tsu])
            ps_set(3, 0, last_out)
            ps_set(3, 512, last_out)
        return last_out

    def vp_dst(tt, oc):
        base = tt * 1040 + 65 * (oc * 8)
        return bass.AP(Vp.tensor, base, [[16640, 128], [65, 8], [1, 64]])

    def ps3d(i):
        return bass.AP(pS[i].tensor, 0, [[1024, 128], [64, 8], [1, 64]])

    def vp_ones_ap():
        return bass.AP(Vp.tensor, 64, [[16640, 128], [1040, 16], [65, 16]])

    def emit_v(stat_fn, rhs_fn, nib, waits0):
        tmem = dve("memset", vp_ones_ap(), 1.0)
        last_e = None
        for tt in range(16):
            for oc in range(2):
                i = (tt * 2 + oc) % 2
                last = None
                for ib in range(nib):
                    w0 = (list(waits0) + ps_waits(i, 0)) if ib == 0 else []
                    last = mm(pS[i][:, 0:512], stat_fn(ib, tt), rhs_fn(ib, oc),
                              start=(ib == 0), stop=(ib == nib - 1),
                              waits=w0, inc="pe" if ib == nib - 1 else None)
                last_e = dve("tensor_copy", vp_dst(tt, oc), ps3d(i), waits=[last, tmem])
                ps_set(i, 0, last_e)
        return last_e

    wslot_free = [None, None, None]
    asub_free = [None] * 8

    ext_bh = sat("ext_bh", [128, 4095], F32, OFF_LAT)

    def ext_diag(c0, wd):
        return bass.AP(ext_bh.tensor, c0 + 127, [[4094, 128], [1, wd]])

    def emit_attention(kind, src_dram, dst_dram, ln_tok):
        scale = SCALE_A if kind == "a" else SCALE_B
        causal = kind == "a"
        store_toks = []

        if kind == "a":
            wq_l, wk_l = wring[0][:, 0:2048], wring[0][:, 2048:4096]
            wv_l, wq_h = wring[1][:, 0:2048], wring[1][:, 2048:4096]
            wk_h, wv_h = wring[2][:, 0:2048], wring[2][:, 2048:4096]
            wload(wq_l, aa_wq_lat, [wslot_free[0]], sem="wd0")
            twk = wload(wk_l, aa_wk_lat, [wslot_free[0]], sem="wd0")
            twq = twk
            wload(wv_l, aa_wv_lat, [wslot_free[1]], sem="wd1")
            twqh = wload(wq_h, aa_wq_h, [wslot_free[1]], sem="wd1")
            twv = twqh
            wload(wk_h, aa_wk_h, [wslot_free[2]], sem="wd2")
            twvh = wload(wv_h, aa_wv_h, [wslot_free[2]], sem="wd2")
            twkh = twvh
            evac = {}
            for (wi, wtok, dst_lat) in ((wq_l, twq, lat_q), (wk_l, twk, lat_k),
                                        (wv_l, twv, lat_v)):
                for ol in range(2):
                    for g in range(QG):
                        i = (ol * QG + g) % 2
                        last = None
                        for ib in range(CB):
                            w0 = ([wtok, ln_tok] + ps_waits(i, 0)) if ib == 0 else []
                            last = mm(pS[i][:, 0:512],
                                      wi[:, ib * 256 + ol * 128: ib * 256 + ol * 128 + 128],
                                      xn[:, ib * T + g * 512: ib * T + g * 512 + 512],
                                      start=(ib == 0), stop=(ib == CB - 1),
                                      waits=w0, inc="pe" if ib == CB - 1 else None)
                        et = act(AF.Copy, dst_lat[:, ol * T + g * 512: ol * T + g * 512 + 512],
                                 pS[i][:, 0:512], waits=[last])
                        ps_set(i, 0, et)
            lat_done = ("act", S.cur("act")[1])
            for ob in range(CB):
                for g in range(QG):
                    i = (ob * QG + g) % 2
                    last = None
                    for ib in range(2):
                        w0 = ([twkh, lat_done] + ps_waits(i, 0)) if ib == 0 else []
                        last = mm(pS[i][:, 0:512],
                                  wk_h[:, ib * 1024 + ob * 128: ib * 1024 + ob * 128 + 128],
                                  lat_k[:, ib * T + g * 512: ib * T + g * 512 + 512],
                                  start=(ib == 0), stop=(ib == 1),
                                  waits=w0, inc="pe" if ib == 1 else None)
                    et = act(AF.Copy, KT[:, ob * T + g * 512: ob * T + g * 512 + 512],
                             pS[i][:, 0:512], waits=[last])
                    ps_set(i, 0, et)
            vput = emit_v(lambda ib, tt: lat_v[:, ib * T + tt * 128: ib * T + tt * 128 + 128],
                          lambda ib, oc: wv_h[:, ib * 1024 + oc * 512: ib * 1024 + oc * 512 + 512],
                          2, [twvh, lat_done])
            qkv_pe = ("pe", S.cur("pe")[1])
            tp0 = wload(wring[0][:, :], aa_wproj, [qkv_pe], col0=0, ncol=512, sem="wd0")
            tp1 = wload(wring[2][:, :], aa_wproj, [qkv_pe], col0=512, ncol=512, sem="wd2")

            def qt_lhs(ib, ob):
                return wq_h[:, ib * 1024 + ob * 128: ib * 1024 + ob * 128 + 128]
            qt_wt = [twqh]
            qt_nib = 2

            def qt_rhs(ib, g):
                return lat_q[:, ib * T + g * 512: ib * T + g * 512 + 512]
            qt_extra = [lat_done]
        else:
            tk0 = S.dma("gpsimd", wring[0][:, :], ab_wkT[0:128, :], waits=[w for w in [wslot_free[0]] if w], sem="wd0")
            tk1 = S.dma("gpsimd", wring[1][:, :], ab_wkT[128:256, :], waits=[w for w in [wslot_free[1]] if w], sem="wd1")
            for ob in range(CB):
                wsl, wtok = (wring[0], tk0) if ob < 4 else (wring[1], tk1)
                obl = ob % 4
                for g in range(QG):
                    i = (ob * QG + g) % 2
                    last = None
                    for ib in range(CB):
                        w0 = ([wtok, ln_tok] + ps_waits(i, 0)) if ib == 0 else []
                        last = mm(pS[i][:, 0:512],
                                  wsl[:, ib * 512 + obl * 128: ib * 512 + obl * 128 + 128],
                                  xn[:, ib * T + g * 512: ib * T + g * 512 + 512],
                                  start=(ib == 0), stop=(ib == CB - 1),
                                  waits=w0, inc="pe" if ib == CB - 1 else None)
                    et = act(AF.Copy, KT[:, ob * T + g * 512: ob * T + g * 512 + 512],
                             pS[i][:, 0:512], waits=[last])
                    ps_set(i, 0, et)
            kt_pe = ("pe", S.cur("pe")[1])
            tv0 = S.dma("gpsimd", wring[0][:, :], ab_wvT[0:128, :], waits=[kt_pe], sem="wd0")
            tv1 = S.dma("gpsimd", wring[1][:, :], ab_wvT[128:256, :], waits=[kt_pe], sem="wd1")
            vput = emit_v(lambda ib, tt: xn[:, ib * T + tt * 128: ib * T + tt * 128 + 128],
                          lambda ib, oc: (wring[0] if oc == 0 else wring[1])[:, ib * 512: ib * 512 + 512],
                          CB, [tv0, tv1, ln_tok])
            v_pe = ("pe", S.cur("pe")[1])
            tq0 = S.dma("gpsimd", wring[0][:, :], ab_wqT[0:128, :], waits=[v_pe], sem="wd0")
            tq1 = S.dma("gpsimd", wring[1][:, :], ab_wqT[128:256, :], waits=[v_pe], sem="wd1")

            def qt_lhs(ib, ob):
                wsl = wring[0] if ob < 4 else wring[1]
                return wsl[:, ib * 512 + (ob % 4) * 128: ib * 512 + (ob % 4) * 128 + 128]
            qt_wt = [tq0, tq1]
            qt_nib = CB
            qt_rhs = lambda ib, g: xn[:, ib * T + g * 512: ib * T + g * 512 + 512]
            qt_extra = []

        kt_all = ("act", S.cur("act")[1])
        ksq_ps = pS[2][0:1, 0:512]
        mmk = None
        sq_hist = {}
        for j in range(32):
            sl_ = ptr[:, (j % 2) * 512:(j % 2) * 512 + 512]
            wsq = [kt_all]
            if j >= 2:
                wsq.append(sq_hist[j - 2])
            sqt = act(AF.Square, sl_, KT[:, j * 512:(j + 1) * 512], waits=wsq)
            w0 = ps_waits(2, 0) if j == 0 else []
            mmk = mm(ksq_ps, ones_col[:, :], sl_, start=(j == 0), stop=(j == 31),
                     waits=[sqt] + w0, inc="pe")
            sq_hist[j] = mmk
        tks = dve("reduce_sum", r_tmp[0:1, 0:1], ksq_ps, AX.X, waits=[mmk])
        ps_set(2, 0, tks)
        tk2 = dve("tensor_scalar", r_tmp[0:1, 0:1], r_tmp[0:1, 0:1],
                  1.0 / (C * T), None, ALU.mult, waits=[tks])
        tkr = act(AF.Ln, scal[0:1, 0:1], r_tmp[0:1, 0:1], waits=[tk2])

        ball_ready = [None, None]
        ball_free = [None, None]
        eb_free = [None]

        def ball_load(h):
            buf = ballb[h % 2]
            sem = "bl0" if h % 2 == 0 else "bl1"
            w = [tok_ext, ("pe", S.cur("pe")[1])]
            if ball_free[h % 2] is not None:
                w.append(ball_free[h % 2])
            if eb_free[0] is not None:
                w.append(eb_free[0])
            src_b = bass.AP(extd.tensor, h * 4095, [[4095, 1], [0, 128], [1, 4095]])
            btok = S.dma("gpsimd", ext_bh[:, :], src_b, waits=w, sem=sem)
            tok = None
            for ci in range(8):
                c0 = ci * 512
                wd_ = 512 if ci < 7 else 384
                tok = S.dma("gpsimd", buf[:, c0:c0 + wd_], ext_diag(c0, wd_),
                            waits=[btok] if ci == 0 else (), sem=sem)
            eb_free[0] = tok
            ball_ready[h % 2] = tok

        prev_qg_dve = None
        prev_recb = None
        prev_recrow = None
        prev_store = None
        proj_toks = None
        slot_teo = [None, None]
        pt_free_carry = [None, None]
        for g in range(QG):
            qts_toks = []
            for ob in range(CB):
                i = ob % 2
                last = None
                for ib in range(qt_nib):
                    w0 = (qt_wt + qt_extra + ps_waits(i, 0) +
                          ([ln_tok] if kind == "b" else [])) if ib == 0 else []
                    last = mm(pS[i][:, 0:512], qt_lhs(ib, ob), qt_rhs(ib, g),
                              start=(ib == 0), stop=(ib == qt_nib - 1),
                              waits=w0, inc="pe" if ib == qt_nib - 1 else None)
                et = act(AF.Copy, qts[:, ob * 512: ob * 512 + 512], pS[i][:, 0:512],
                         waits=[last])
                ps_set(i, 0, et)
                qts_toks.append(et)
            qts_done = qts_toks[-1]

            qsq_ps = pS[2][0:1, 0:512]
            mmq = None
            sq2 = {}
            for j in range(CB):
                sl_ = ptr[:, (j % 2) * 512:(j % 2) * 512 + 512]
                wsq = [qts_done]
                if pt_free_carry[j % 2] is not None:
                    wsq.append(pt_free_carry[j % 2])
                if j >= 2:
                    wsq.append(sq2[j - 2])
                sqt = act(AF.Square, sl_, qts[:, j * 512:(j + 1) * 512], waits=wsq)
                w0 = ps_waits(2, 0) if j == 0 else []
                mmq = mm(qsq_ps, ones_col[:, :], sl_, start=(j == 0), stop=(j == CB - 1),
                         waits=[sqt] + w0, inc="pe")
                sq2[j] = mmq
            tqs = dve("reduce_sum", r_tmp[0:1, 0:1], qsq_ps, AX.X, waits=[mmq])
            ps_set(2, 0, tqs)
            tq2 = dve("tensor_scalar", r_tmp[0:1, 0:1], r_tmp[0:1, 0:1],
                      1.0 / (C * 512), None, ALU.mult, waits=[tqs])
            tql = act(AF.Ln, scal[0:1, 1:2], r_tmp[0:1, 0:1], waits=[tq2])
            tpr = dve("tensor_tensor", scal[0:1, 2:3], scal[0:1, 0:1], scal[0:1, 1:2],
                      ALU.add, waits=[tql, tkr])
            th = dve("tensor_scalar", scal[0:1, 2:3], scal[0:1, 2:3], 0.5, None,
                     ALU.mult, waits=[tpr])
            tex = act(AF.Exp, scal[0:1, 2:3], scal[0:1, 2:3], waits=[th])
            tbs = dve("tensor_scalar", scal[0:1, 2:3], scal[0:1, 2:3], -scale * DH, None,
                      ALU.mult, waits=[tex])
            bps = pS[3][:, 512:513]
            tbc = mm(bps, ones_row[:, :], scal[0:1, 2:3], start=True, stop=True,
                     waits=[tbs] + ps_waits(3, 512), inc="pe")
            tbcol = dve("tensor_copy", biascol[:, :], bps, waits=[tbc])
            ps_set(3, 512, tbcol)

            if kind == "b":
                if prev_qg_dve is not None:
                    ball_free[0] = prev_qg_dve
                ball_load(0)
            nkb = 4 * g + 4 if causal else 16
            sT_free = [None, None]
            pt_free = [pt_free_carry[0], pt_free_carry[1], None, None]
            pv_read = [None, None]
            bv_read = [None, None]
            head_dve = []
            for h in range(NH):
                cbh, po, par = h // 2, 64 * (h % 2), h % 2
                if kind == "b" and h + 1 < NH:
                    ball_load(h + 1)

                pv_ps = pS[2][:, par * 512: par * 512 + 1024 - 512]
                bv_ps = pS[3][:, par * 512: par * 512 + 1024 - 512]
                pv_last = bv_last = None
                pairs = list(range(0, nkb, 2))

                def emit_spair(kb0):
                    si = (kb0 // 2) % 2
                    npair = min(kb0 + 2, nkb) - kb0
                    tokmm = None
                    for j in range(npair):
                        kb = kb0 + j
                        w0 = []
                        if j == 0:
                            w0 = [qts_done, tbcol]
                            if sT_free[si] is not None:
                                w0.append(sT_free[si])
                            w0 += ps_waits(si, "full")
                        tokmm = mm(pS[si][:, j * 512: j * 512 + 512],
                                   KT[po:po + 64, cbh * T + kb * 128: cbh * T + kb * 128 + 128],
                                   qts[po:po + 64, cbh * 512: cbh * 512 + 512],
                                   start=True, stop=True, waits=w0, inc="pe")
                    wexp = [tokmm]
                    for jj in range(npair):
                        sl_ = (kb0 + jj) % 4
                        if pt_free[sl_] is not None:
                            wexp.append(pt_free[sl_])
                    p0 = (kb0 % 4) * 512
                    etok = act(AF.Exp, ptr[:, p0:p0 + npair * 512], pS[si][:, 0:npair * 512],
                               waits=wexp, scale=scale, bias=biascol[:, :])
                    sT_free[si] = etok
                    ps_set(si, "full", etok)
                    for jj in range(npair):
                        exp_toks[kb0 + jj] = etok

                def emit_pvpair(kb0):
                    nonlocal pv_last, bv_last
                    npair = min(kb0 + 2, nkb) - kb0
                    for kb in range(kb0, kb0 + npair):
                        p0 = (kb % 4) * 512
                        ptok = exp_toks[kb]
                        if causal and kb >= 4 * g:
                            j = kb - 4 * g
                            ptok = dve("tensor_tensor", ptr[:, p0:p0 + 512],
                                       ptr[:, p0:p0 + 512],
                                       maskW[:, 384 - 128 * j: 896 - 128 * j], ALU.mult,
                                       waits=[ptok, tok_msk])
                        wt = [ptok, vput]
                        if kb == 0:
                            wt += ps_waits(2, par * 512)
                        pv_last = mm(pv_ps[0:65, :],
                                     Vp[:, kb * 1040 + 65 * h: kb * 1040 + 65 * h + 65],
                                     ptr[:, p0:p0 + 512], start=(kb == 0),
                                     stop=(kb == nkb - 1), waits=wt, inc="pe")
                        pt_free[kb % 4] = pv_last

                exp_toks = {}
                prev_pairk = None
                for kb0 in pairs:
                    emit_spair(kb0)
                    if prev_pairk is not None:
                        emit_pvpair(prev_pairk)
                    prev_pairk = kb0
                emit_pvpair(prev_pairk)
                if kind == "b":
                    for tb in range(16):
                        z0 = 512 * g - 128 * tb + 1920
                        wtb = [ball_ready[h % 2], vput]
                        if tb == 0:
                            wtb += ps_waits(3, par * 512)
                        bv_last = mm(bv_ps[0:64, :],
                                     Vp[:, tb * 1040 + 65 * h: tb * 1040 + 65 * h + 64],
                                     ballb[h % 2][:, z0:z0 + 512],
                                     start=(tb == 0), stop=(tb == 15),
                                     waits=wtb, inc="pe")
                if kind == "b":
                    ball_free[h % 2] = bv_last
                wrc = [pv_last]
                if prev_recb is not None:
                    wrc.append(prev_recb)
                trc = dve("reciprocal", r_recip, pv_ps[64:65, :], waits=wrc)
                wbc = [trc]
                if prev_recrow is not None:
                    wbc.append(prev_recrow)
                tbc2 = S.dma("gpsimd", recipB[:, :],
                             bass.AP(rows.tensor, 0, [[1536, 1], [0, 64], [1, 512]]),
                             waits=wbc, sem="rb")
                prev_recrow = None
                if kind == "a":
                    tep = dve("tensor_tensor", ao_bf[po:po + 64, cbh * 512: cbh * 512 + 512],
                              pv_ps[0:64, :], recipB[:, :], ALU.mult, waits=[tbc2])
                    ps_set(2, par * 512, tep)
                    prev_recb = tep
                else:
                    tmu = dve("tensor_tensor", tsub[0:64, :], pv_ps[0:64, :], recipB[:, :],
                              ALU.mult, waits=[tbc2])
                    ps_set(2, par * 512, tmu)
                    prev_recb = tmu
                    if DBG_NO_BIAS:
                        tep = dve("tensor_copy", ao_f[po:po + 64, cbh * 512: cbh * 512 + 512],
                                  tsub[0:64, :], waits=[tmu, bv_last])
                    else:
                        tep = dve("tensor_tensor", ao_f[po:po + 64, cbh * 512: cbh * 512 + 512],
                                  tsub[0:64, :], bv_ps[0:64, :], ALU.add, waits=[tmu, bv_last])
                    ps_set(3, par * 512, tep)
                prev_recrow = tbc2
                head_dve.append(tep)
            heads_done = head_dve[-1]

            if kind == "b" and DBG_RAW:
                for cb in range(CB):
                    stok = S.dma("sync",
                                 dst_dram[cb * 128:(cb + 1) * 128, g * 512:g * 512 + 512],
                                 ao_f[:, cb * 512: cb * 512 + 512],
                                 waits=[heads_done], sem="st")
                    store_toks.append(stok)
                prev_qg_dve = ("dve", S.cur("dve")[1])
                continue
            if kind == "a":
                if proj_toks is None:
                    proj_toks = (tp0, tp1)
                for ob in range(CB):
                    i = ob % 2
                    wsl = wring[0] if ob < 4 else wring[2]
                    last = None
                    for ib in range(CB):
                        w0 = ([proj_toks[0], proj_toks[1], heads_done] +
                              ps_waits(i, 0)) if ib == 0 else []
                        last = mm(pS[i][:, 0:512],
                                  wsl[:, ib * 512 + (ob % 4) * 128: ib * 512 + (ob % 4) * 128 + 128],
                                  ao_bf[:, ib * 512: ib * 512 + 512],
                                  start=(ib == 0), stop=(ib == CB - 1),
                                  waits=w0, inc="pe" if ib == CB - 1 else None)
                    sl = xsr[ob % 2]
                    wlt = [last]
                    if slot_teo[ob % 2] is not None:
                        wlt.append(slot_teo[ob % 2])
                    ltok = S.dma("sync", sl[:, :],
                                 src_dram[ob * 128:(ob + 1) * 128, g * 512:g * 512 + 512],
                                 waits=wlt, sem=f"xl{ob % 2}")
                    weo = [last, ltok, tok_cst]
                    if prev_store is not None:
                        weo.append(prev_store)
                    teo = dve("scalar_tensor_tensor", t2sb[:, :], pS[i][:, 0:512],
                              bprojc[:, ob:ob + 1], sl[:, :], ALU.add, ALU.add, waits=weo)
                    slot_teo[ob % 2] = teo
                    ps_set(i, 0, teo)
                    prev_store = S.dma("sync",
                                       dst_dram[ob * 128:(ob + 1) * 128, g * 512:g * 512 + 512],
                                       t2sb[:, :], waits=[teo], sem="st")
                    store_toks.append(prev_store)
            else:
                sum_ps = pS[2][0:1, 0:512]
                sumsq_ps = pS[2][0:1, 512:1024]
                mm_sum = mm_sq = None
                for cb in range(CB):
                    sq = silu_sb[:, (cb % 2) * 512:(cb % 2) * 512 + 512]
                    wsq = [heads_done]
                    if mm_sq is not None and cb >= 2:
                        wsq.append(mm_sq)
                    last_sq2 = act(AF.Square, sq, ao_f[:, cb * 512: cb * 512 + 512],
                                   waits=wsq)
                    w0 = ps_waits(2, 0) if cb == 0 else []
                    mm_sum = mm(sum_ps, ones_col_f[:, :], ao_f[:, cb * 512: cb * 512 + 512],
                                start=(cb == 0), stop=(cb == CB - 1),
                                waits=[heads_done] + w0, inc="pe")
                    w0 = ps_waits(2, 512) if cb == 0 else []
                    mm_sq = mm(sumsq_ps, ones_col_f[:, :], sq, start=(cb == 0),
                               stop=(cb == CB - 1), waits=[last_sq2] + w0, inc="pe")
                tmean = dve("tensor_scalar", r_mean, sum_ps, 1.0 / C, None, ALU.mult,
                            waits=[mm_sum, mm_sq])
                tmsq = dve("tensor_tensor", r_tmp, r_mean, r_mean, ALU.mult, waits=[tmean])
                tvar = dve("scalar_tensor_tensor", r_rstd, sumsq_ps, 1.0 / C, r_tmp,
                           ALU.mult, ALU.subtract, waits=[tmsq])
                ps_set(2, 0, tvar)
                ps_set(2, 512, tvar)
                tvar = dve("tensor_scalar", r_rstd, r_rstd, LN_EPS, None, ALU.add,
                           waits=[tvar])
                tln = act(AF.Ln, r_tmp, r_rstd, waits=[tvar])
                tno = dve("tensor_scalar", r_tmp, r_tmp, -0.5, None, ALU.mult, waits=[tln])
                texp = act(AF.Exp, r_rstd, r_tmp, waits=[tno])
                meanB = pS[3][:, 0:512]
                rstdB = pS[3][:, 512:1024]
                tbm = mm(meanB, ones_row[:, :], r_mean, start=True, stop=True,
                         waits=[tmsq] + ps_waits(3, 0), inc="pe")
                tbr = mm(rstdB, ones_row[:, :], r_rstd, start=True, stop=True,
                         waits=[texp] + ps_waits(3, 512), inc="pe")
                lastt = None
                for cb in range(CB):
                    sl = xsr[cb % 2]
                    ltok = S.dma("sync", sl[:, :],
                                 src_dram[cb * 128:(cb + 1) * 128, g * 512:g * 512 + 512],
                                 waits=[lastt] if lastt else [], sem=f"xl{cb % 2}")
                    tsu = dve("tensor_tensor", tsub[:, :], ao_f[:, cb * 512: cb * 512 + 512],
                              meanB, ALU.subtract, waits=[tbm, tbr])
                    tmu2 = dve("tensor_tensor", tsub[:, :], tsub[:, :], rstdB, ALU.mult,
                               waits=[tsu])
                    tgb = dve("tensor_scalar", tsub[:, :], tsub[:, :], gammac[:, cb:cb + 1],
                              betac[:, cb:cb + 1], ALU.mult, op1=ALU.add,
                              waits=[tmu2, tok_cst])
                    weo = [tgb, ltok]
                    if prev_store is not None:
                        weo.append(prev_store)
                    teo = dve("tensor_tensor", t2sb[:, :], tsub[:, :], sl[:, :], ALU.add,
                              waits=weo)
                    lastt = teo
                    prev_store = S.dma("sync",
                                       dst_dram[cb * 128:(cb + 1) * 128, g * 512:g * 512 + 512],
                                       t2sb[:, :], waits=[teo], sem="st")
                    store_toks.append(prev_store)
                ps_set(3, 0, lastt)
                ps_set(3, 512, lastt)
            prev_qg_dve = ("dve", S.cur("dve")[1])
            pt_free_carry[0] = pt_free[0]
            pt_free_carry[1] = pt_free[1]
        pe_end = ("pe", S.cur("pe")[1])
        for _sl in range(3):
            wslot_free[_sl] = pe_end
        return store_toks

    def emit_ffn(src_dram, dst_dram, pre_tok):
        store_toks = []
        prev_store = None
        slot_teo2 = [None, None]
        for ch in range(2):
            tb0 = ch * 1024
            ln_tok = emit_ln(src_dram, xnc, 1024, tb0, 1024,
                             after_toks=[pre_tok] if pre_tok else [])
            slab = {}

            def loadA(ob):
                for ti, wsrc in ((0, f_w1T), (1, f_w2T)):
                    k = (2 * ob + ti) % 8
                    sub = wring[k // 4][:, (k % 4) * 1024:(k % 4) * 1024 + 1024]
                    w = []
                    if asub_free[k] is not None:
                        w.append(asub_free[k])
                    if wslot_free[k // 4] is not None:
                        w.append(wslot_free[k // 4])
                    tok = S.dma("gpsimd", sub, wsrc[ob * 128:(ob + 1) * 128, :],
                                waits=w, sem=f"wa{k}")
                    slab[(ti, ob)] = (tok, k)
            loadA(0)
            loadA(1)
            loadA(2)
            silu_free = None
            aa_last = None
            prev_pair = [None, None]
            for ob in range(IBH):
                if ob + 3 < IBH:
                    loadA(ob + 3)
                par = ob % 2
                u1, u2 = pS[2 * par], pS[2 * par + 1]
                t1, k1 = slab[(0, ob)]
                t2, k2 = slab[(1, ob)]
                last1 = last2 = None
                for (u, tke, kk, which) in ((u1, t1, k1, 1), (u2, t2, k2, 2)):
                    sub = wring[kk // 4][:, (kk % 4) * 1024:(kk % 4) * 1024 + 1024]
                    last = None
                    for ib in range(CB):
                        for hf in range(2):
                            w0 = []
                            if ib == 0 and hf == 0:
                                w0 = [tke, ln_tok] + ps_waits(2 * par + (which - 1), "full")
                                if prev_pair[par] is not None:
                                    w0.append(prev_pair[par])
                            last = mm(u[:, hf * 512: hf * 512 + 512],
                                      sub[:, ib * 128: ib * 128 + 128],
                                      xnc[:, ib * 1024 + hf * 512: ib * 1024 + hf * 512 + 512],
                                      start=(ib == 0), stop=(ib == CB - 1),
                                      waits=w0,
                                      inc="pe" if (ib == CB - 1 and hf == 1) else None)
                    if which == 1:
                        last1 = last
                    else:
                        last2 = last
                wsl_ = [last1]
                if silu_free is not None:
                    wsl_.append(silu_free)
                tsi = act(AF.Silu, silu_sb[:, :], u1[:, :], waits=wsl_,
                          scale=s1c[:, ob:ob + 1])
                taa = dve("scalar_tensor_tensor", AAt[:, ob * 1024: ob * 1024 + 1024],
                          u2[:, :], s2c[:, ob:ob + 1], silu_sb[:, :],
                          ALU.mult, ALU.mult, waits=[tsi, last2])
                silu_free = taa
                aa_last = taa
                prev_pair[par] = taa
                ps_set(2 * par, "full", taa)
                ps_set(2 * par + 1, "full", taa)
                petok = ("pe", S.cur("pe")[1])
                asub_free[k1] = petok
                asub_free[k2] = petok
            petok = ("pe", S.cur("pe")[1])
            wslot_free[0] = petok
            wslot_free[1] = petok

            def hidden(wsrc, scol, src_t, src_tok, dst_t):
                prevp = [None, None]
                out_last = None
                for ob in range(IBH):
                    sl = (ob + 2) % 3
                    w = [wslot_free[sl]] if wslot_free[sl] else []
                    wtok = S.dma("gpsimd", wring[sl][:, :],
                                 wsrc[ob * 128:(ob + 1) * 128, :], waits=w, sem=f"wd{sl}")
                    par = ob % 2
                    ps = pS[2 * par]
                    last = None
                    for ib in range(IBH):
                        for hf in range(2):
                            w0 = []
                            if ib == 0 and hf == 0:
                                w0 = [wtok, src_tok] + ps_waits(2 * par, "full")
                                if prevp[par] is not None:
                                    w0.append(prevp[par])
                            last = mm(ps[:, hf * 512: hf * 512 + 512],
                                      wring[sl][:, ib * 128: ib * 128 + 128],
                                      src_t[:, ib * 1024 + hf * 512: ib * 1024 + hf * 512 + 512],
                                      start=(ib == 0), stop=(ib == IBH - 1),
                                      waits=w0,
                                      inc="pe" if (ib == IBH - 1 and hf == 1) else None)
                    wslot_free[sl] = last
                    tgo = act(AF.Gelu, dst_t[:, ob * 1024: ob * 1024 + 1024], ps[:, :],
                              waits=[last], scale=scol[:, ob:ob + 1])
                    prevp[par] = tgo
                    ps_set(2 * par, "full", tgo)
                    out_last = tgo
                return out_last
            g1_tok = hidden(f_h1T, h1sc, AAt, aa_last, GG)
            g2_tok = hidden(f_h2T, h2sc, GG, g1_tok, AAt)

            prevp = [None, None]
            for ob in range(CB):
                sl = (ob + 2) % 3
                w = [wslot_free[sl]] if wslot_free[sl] else []
                wtok = S.dma("gpsimd", wring[sl][:, :],
                             f_w3T[ob * 128:(ob + 1) * 128, :], waits=w, sem=f"wd{sl}")
                par = ob % 2
                ps = pS[2 * par]
                last = None
                for ib in range(IBH):
                    for hf in range(2):
                        w0 = []
                        if ib == 0 and hf == 0:
                            w0 = [wtok, g2_tok] + ps_waits(2 * par, "full")
                            if prevp[par] is not None:
                                w0.append(prevp[par])
                        last = mm(ps[:, hf * 512: hf * 512 + 512],
                                  wring[sl][:, ib * 128: ib * 128 + 128],
                                  AAt[:, ib * 1024 + hf * 512: ib * 1024 + hf * 512 + 512],
                                  start=(ib == 0), stop=(ib == IBH - 1),
                                  waits=w0,
                                  inc="pe" if (ib == IBH - 1 and hf == 1) else None)
                wslot_free[sl] = last
                teo_last = None
                for hf in range(2):
                    sl2 = xsr[(ob * 2 + hf) % 2]
                    t0 = tb0 + hf * 512
                    wlt = [last]
                    if slot_teo2[(ob * 2 + hf) % 2] is not None:
                        wlt.append(slot_teo2[(ob * 2 + hf) % 2])
                    ltok = S.dma("sync", sl2[:, :],
                                 src_dram[ob * 128:(ob + 1) * 128, t0:t0 + 512],
                                 waits=wlt, sem=f"xl{(ob * 2 + hf) % 2}")
                    weo = [last, ltok]
                    if prev_store is not None:
                        weo.append(prev_store)
                    teo = dve("scalar_tensor_tensor", t2sb[:, :],
                              ps[:, hf * 512: hf * 512 + 512],
                              w3sc[:, ob:ob + 1], sl2[:, :], ALU.mult, ALU.add, waits=weo)
                    slot_teo2[(ob * 2 + hf) % 2] = teo
                    teo_last = teo
                    prev_store = S.dma("sync",
                                       dst_dram[ob * 128:(ob + 1) * 128, t0:t0 + 512],
                                       t2sb[:, :], waits=[teo], sem="st")
                    store_toks.append(prev_store)
                prevp[par] = teo_last
                ps_set(2 * par, "full", teo_last)
        return store_toks

    kindseq = KINDSEQ
    last_tok = None
    for i in range(n_sub):
        src = x_in if i == 0 else xres
        dst = y_out if i == n_sub - 1 else xres
        if kindseq[i] in ("attn_a", "attn_b"):
            after = [t for t in ([last_tok] if last_tok else []) + list(setup_tok)]
            lt = emit_ln(src, xn, T, 0, T, after_toks=after)
            stoks = emit_attention("a" if kindseq[i] == "attn_a" else "b", src, dst, lt)
        else:
            stoks = emit_ffn(src, dst, last_tok)
        last_tok = stoks[-1]

    S.add("sync", None, waits=[S.cur("st")])

    with nc.Block() as block:
        S.emit(block, sems)
    stack.close()
    return nc


_NC_CACHE = None


def kernel(**inputs):
    global _NC_CACHE

    def T_(a):
        return np.ascontiguousarray(np.asarray(a, np.float32).T)

    def pcol(v, nb):
        return np.ascontiguousarray(np.asarray(v, np.float32).reshape(nb, 128).T)

    def perm(w, nob, ow, nib):
        w = np.asarray(w, np.float32).reshape(nob, ow, nib, 128)
        return np.ascontiguousarray(w.transpose(0, 3, 2, 1).reshape(nob * 128, nib * ow))

    x = np.asarray(inputs["L0_attn_a_x"], np.float32)
    B = x.shape[0]
    common = {
        "aa_wq_lat": np.ascontiguousarray(inputs["L0_attn_a_wq_lat"], dtype=np.float32),
        "aa_wk_lat": np.ascontiguousarray(inputs["L0_attn_a_wk_lat"], dtype=np.float32),
        "aa_wv_lat": np.ascontiguousarray(inputs["L0_attn_a_wv_lat"], dtype=np.float32),
        "aa_wq_h": np.ascontiguousarray(inputs["L0_attn_a_wq_heads"], dtype=np.float32),
        "aa_wk_h": np.ascontiguousarray(inputs["L0_attn_a_wk_heads"], dtype=np.float32),
        "aa_wv_h": np.ascontiguousarray(inputs["L0_attn_a_wv_heads"], dtype=np.float32),
        "aa_wproj": np.ascontiguousarray(inputs["L0_attn_a_w_proj"], dtype=np.float32),
        "aa_bproj": pcol(inputs["L0_attn_a_b_proj"], CB),
        "ab_wqT": perm(inputs["L0_attn_b_Wq"], 2, 512, 8),
        "ab_wkT": perm(inputs["L0_attn_b_Wk"], 2, 512, 8),
        "ab_wvT": perm(inputs["L0_attn_b_Wv"], 2, 512, 8),
        "ab_tab": np.ascontiguousarray(inputs["L0_attn_b_bias_table"], dtype=np.float32),
        "ab_gamma": pcol(inputs["L0_attn_b_ln_gamma"], CB),
        "ab_beta": pcol(inputs["L0_attn_b_ln_beta"], CB),
        "f_w1T": perm(inputs["L0_ffn_w1_w"], 32, 128, 8),
        "f_w2T": perm(inputs["L0_ffn_w2_w"], 32, 128, 8),
        "f_h1T": perm(inputs["L0_ffn_h1_w"], 32, 128, 32),
        "f_h2T": perm(inputs["L0_ffn_h2_w"], 32, 128, 32),
        "f_w3T": perm(inputs["L0_ffn_w3_w"], 8, 128, 32),
        "f_s1": pcol(inputs["L0_ffn_w1_s"], IBH),
        "f_s2": pcol(inputs["L0_ffn_w2_s"], IBH),
        "f_h1s": pcol(inputs["L0_ffn_h1_s"], IBH),
        "f_h2s": pcol(inputs["L0_ffn_h2_s"], IBH),
        "f_w3s": pcol(inputs["L0_ffn_w3_s"], CB),
        "c_maskW": (np.arange(896)[None, :] >= np.arange(128)[:, None] + 384
                    ).astype(np.float32),
        "c_ones_row": np.ones((1, 128), np.float32),
        "c_ones_col": np.ones((128, 1), np.float32),
    }
    in_maps = []
    for b in range(B):
        m = dict(common)
        m["x"] = T_(x[b])
        in_maps.append(m)

    if _NC_CACHE is None:
        _NC_CACHE = build_nc()
    res = run_bass_kernel_spmd(_NC_CACHE, in_maps, core_ids=list(range(B)))
    out = np.stack([np.asarray(r["y"], np.float32).T for r in res.results])
    return np.ascontiguousarray(out)


if __name__ == "__main__":
    nc = build_nc()
    ni = sum(len(q) for q in nc.__dict__.get("_dummy", {}) or []) if False else None
    print("built ok")


# version 07
# speedup vs baseline: 1.0255x; 1.0255x over previous
import numpy as np
import ml_dtypes
import concourse.bass as bass
import concourse.mybir as mybir
from concourse.bass_utils import run_bass_kernel_spmd

F32 = mybir.dt.float32
BF16 = mybir.dt.bfloat16
AF = mybir.ActivationFunctionType
ALU = mybir.AluOpType
AX = mybir.AxisListType

T = 2048
C = 1024
HID = 4096
NH = 16
DH = 64
LAT = 256
QG = 4
CB = C // 128
IBH = HID // 128
LN_EPS = 1e-5
SCALE_A = 0.125
SCALE_B = 0.03125
N_SUB = 4
KINDSEQ = ["attn_a", "ffn", "attn_b", "ffn"]
DBG_NO_BIAS = False
DBG_RAW = False

ENGINES = ("sync", "gpsimd", "tensor", "vector", "scalar")

BASE = 16512
OFF_XN = BASE + 0
OFF_QTS = BASE + 32768
OFF_LAT = BASE + 40960
OFF_AA = BASE + 0
OFF_KT = BASE + 65536
OFF_VP = BASE + 98304
OFF_GG = BASE + 65536
OFF_AO = BASE + 131584
OFF_WR = BASE + 147968
OFF_XS = BASE + 172544
OFF_BALL = BASE + 178688
OFF_PT = BASE + 194560
OFF_MASK = BASE + 198656
OFF_RECB = BASE + 200448
OFF_ROWS = BASE + 202496
OFF_CONST = BASE + 208640
SBUF_END = 229376


def ceil32(x):
    return (x + 31) // 32 * 32


class Sched:
    def __init__(self):
        self.q = {e: [] for e in ENGINES}
        self.c = {}

    def cur(self, sem):
        return (sem, self.c.get(sem, 0))

    def add(self, eng, fn, waits=(), inc=None, amt=1):
        tok = None
        for (s, v) in waits:
            assert v <= self.c.get(s, 0), f"wait on future token {s}={v}"
        if inc is not None:
            self.c[inc] = self.c.get(inc, 0) + amt
            tok = (inc, self.c[inc])
        self.q[eng].append((fn, tuple(waits), inc, amt))
        return tok

    def dma(self, eng, out, in_, waits=(), sem=None, **kw):
        def fn(e, out=out, in_=in_, kw=kw):
            return e.dma_start(out=out, in_=in_, **kw)
        return self.add(eng, fn, waits=waits, inc=sem, amt=16)

    def emit(self, block, sems):
        for name in ENGINES:
            ops = self.q[name]

            def make(ops):
                def body(eng):
                    waited = {}
                    for fn, waits, inc, amt in ops:
                        for (s, v) in waits:
                            if v > 0 and waited.get(s, 0) < v:
                                eng.wait_ge(sems[s], v)
                                waited[s] = v
                        if fn is None:
                            continue
                        inst = fn(eng)
                        if inc is not None:
                            inst.then_inc(sems[inc], amt)
                return body
            getattr(block, name)(make(ops))


def build_nc(n_sub=None):
    if n_sub is None:
        n_sub = N_SUB
    nc = bass.Bass("TRN2", target_bir_lowering=False)

    def din(name, shape):
        return nc.dram_tensor(name, shape, F32, kind="ExternalInput").ap()

    x_in = din("x", [C, T])
    dbgd = nc.dram_tensor("dbg", [NH, T], F32, kind="ExternalOutput").ap()
    y_out = nc.dram_tensor("y", [C, T], F32, kind="ExternalOutput").ap()
    xres = nc.dram_tensor("xres", [C, T], F32, kind="Internal").ap()
    extd = nc.dram_tensor("extd", [16, 4095], F32, kind="Internal").ap()

    aa_wq_lat = din("aa_wq_lat", [C, LAT])
    aa_wk_lat = din("aa_wk_lat", [C, LAT])
    aa_wv_lat = din("aa_wv_lat", [C, LAT])
    aa_wq_h = din("aa_wq_h", [LAT, C])
    aa_wk_h = din("aa_wk_h", [LAT, C])
    aa_wv_h = din("aa_wv_h", [LAT, C])
    aa_wproj = din("aa_wproj", [C, C])
    aa_bproj = din("aa_bproj", [128, CB])
    ab_wqT = din("ab_wqT", [2 * 128, 8 * 512])
    ab_wkT = din("ab_wkT", [2 * 128, 8 * 512])
    ab_wvT = din("ab_wvT", [2 * 128, 8 * 512])
    ab_tab = din("ab_tab", [2047, NH])
    ab_gamma = din("ab_gamma", [128, CB])
    ab_beta = din("ab_beta", [128, CB])
    f_w1T = din("f_w1T", [32 * 128, 8 * 128])
    f_w2T = din("f_w2T", [32 * 128, 8 * 128])
    f_h1T = din("f_h1T", [32 * 128, 32 * 128])
    f_h2T = din("f_h2T", [32 * 128, 32 * 128])
    f_w3T = din("f_w3T", [8 * 128, 32 * 128])
    f_s1 = din("f_s1", [128, IBH])
    f_s2 = din("f_s2", [128, IBH])
    f_h1s = din("f_h1s", [128, IBH])
    f_h2s = din("f_h2s", [128, IBH])
    f_w3s = din("f_w3s", [128, CB])
    c_maskW = din("c_maskW", [128, 896])
    c_ones_row = din("c_ones_row", [1, 128])
    c_ones_col = din("c_ones_col", [128, 1])

    def sat(name, shape, dt, off):
        return nc.alloc_sbuf_tensor_at(name, shape, dt, offset=off).ap()

    xn = sat("xn", [128, CB * T], BF16, OFF_XN)
    qts = sat("qts", [128, CB * 512], BF16, OFF_QTS)
    lat_q = sat("lat_q", [128, 2 * T], BF16, OFF_LAT)
    lat_k = sat("lat_k", [128, 2 * T], BF16, OFF_LAT + 8192)
    lat_v = sat("lat_v", [128, 2 * T], BF16, OFF_LAT + 16384)
    AAt = sat("AAt", [128, IBH * 1024], BF16, OFF_AA)
    KT = sat("KT", [128, CB * T], BF16, OFF_KT)
    Vp = sat("Vp", [128, 16 * 1040], BF16, OFF_VP)
    GG = sat("GG", [128, IBH * 1024], BF16, OFF_GG)
    ext_sb = sat("ext_sb", [16, 4095], F32, OFF_AA)
    ao_f = sat("ao_f", [128, CB * 512], F32, OFF_AO)
    ao_bf = sat("ao_bf", [128, CB * 512], BF16, OFF_AO)
    xnc = sat("xnc", [128, CB * 1024], BF16, OFF_AO)
    wring = [sat(f"wr{i}", [128, 4096], BF16, OFF_WR + i * 8192) for i in range(3)]
    xsr = [sat(f"xs{i}", [128, 512], F32, OFF_XS + i * 2048) for i in range(2)]
    ballb = [sat(f"ball{i}", [128, 3968], BF16, OFF_BALL + i * 7936) for i in range(2)]
    ptr = sat("ptr", [128, 4 * 512], BF16, OFF_PT)
    silu_sb = sat("silu_sb", [128, 1024], F32, OFF_PT)
    maskW = sat("maskW", [128, 896], BF16, OFF_MASK)
    recipB = sat("recipB", [64, 512], F32, OFF_RECB)
    rows = sat("rows", [1, 1536], F32, OFF_ROWS)
    co = OFF_CONST
    ones_row = sat("ones_row", [1, 128], F32, co); co = ceil32(co + 512)
    ones_col = sat("ones_col", [128, 1], BF16, co); co = ceil32(co + 4)
    ones_col_f = sat("ones_col_f", [128, 1], F32, co); co = ceil32(co + 4)
    s1c = sat("s1c", [128, IBH], F32, co); co = ceil32(co + IBH * 4)
    s2c = sat("s2c", [128, IBH], F32, co); co = ceil32(co + IBH * 4)
    h1sc = sat("h1sc", [128, IBH], F32, co); co = ceil32(co + IBH * 4)
    h2sc = sat("h2sc", [128, IBH], F32, co); co = ceil32(co + IBH * 4)
    w3sc = sat("w3sc", [128, CB], F32, co); co = ceil32(co + CB * 4)
    bprojc = sat("bprojc", [128, CB], F32, co); co = ceil32(co + CB * 4)
    gammac = sat("gammac", [128, CB], F32, co); co = ceil32(co + CB * 4)
    betac = sat("betac", [128, CB], F32, co); co = ceil32(co + CB * 4)
    biascol = sat("biascol", [128, 1], F32, co); co = ceil32(co + 4)
    kmaxc = sat("kmaxc", [128, 1], F32, co); co = ceil32(co + 4)
    qmaxc = sat("qmaxc", [128, 1], F32, co); co = ceil32(co + 4)
    scr_row = sat("scr_row", [1, 128], F32, co); co = ceil32(co + 512)
    scal = sat("scal", [1, 4], F32, co); co = ceil32(co + 16)
    tsub = sat("tsub", [128, 512], F32, co); co = ceil32(co + 2048)
    t2sb = sat("t2sb", [128, 512], F32, OFF_XS + 4096)
    assert co <= SBUF_END, f"sbuf overflow {co}"

    pS = [nc.alloc_psum_tensor(f"pS{i}", [128, 1024], F32).ap() for i in range(4)]

    semnames = (("ld", "wd", "bd", "st", "pe", "act", "dve", "bl0", "bl1", "rb",
                 "xl0", "xl1") + tuple(f"wd{i}" for i in range(3))
                + tuple(f"wa{i}" for i in range(8)))
    from contextlib import ExitStack
    stack = ExitStack()
    sems = {s: stack.enter_context(nc.semaphore(s)) for s in semnames}

    S = Sched()
    r_mean = rows[0:1, 0:512]
    r_rstd = rows[0:1, 512:1024]
    r_tmp = rows[0:1, 1024:1536]
    r_recip = rows[0:1, 0:512]

    psfree = {}

    def ps_waits(i, part):
        out = []
        if part == "full":
            keys = [(i, 0), (i, 512), (i, "full")]
        else:
            keys = [(i, part), (i, "full")]
        for k in keys:
            if psfree.get(k) is not None:
                out.append(psfree[k])
        return out

    def ps_set(i, part, tok):
        psfree[(i, part)] = tok

    def mm(out, lhsT, rhs, start, stop, waits=(), inc=None):
        def fn(e, out=out, lhsT=lhsT, rhs=rhs, start=start, stop=stop):
            return e.matmul(out, lhsT, rhs, start=start, stop=stop)
        return S.add("tensor", fn, waits=waits, inc=inc)

    def act(fn_type, out, in_, waits=(), scale=1.0, bias=0.0):
        def fn(e, out=out, in_=in_, fn_type=fn_type, scale=scale, bias=bias):
            return e.activation(out, in_, fn_type, bias=bias, scale=scale)
        return S.add("scalar", fn, waits=waits, inc="act")

    def dve(method, *args, waits=(), **kw):
        def fn(e, method=method, args=args, kw=kw):
            return getattr(e, method)(*args, **kw)
        return S.add("vector", fn, waits=waits, inc="dve")

    def wload(dst, src, wait_toks, col0=0, ncol=None, sem="wd"):
        if ncol is None:
            ncol = src.shape[1]
        src_r = src.rearrange("(ib p) o -> p ib o", p=128)[:, :, col0:col0 + ncol]
        return S.dma("gpsimd", dst, src_r, waits=[w for w in wait_toks if w], sem=sem)

    S.dma("sync", ones_row[:, :], c_ones_row[:, :], sem="ld")
    S.dma("sync", s1c[:, :], f_s1[:, :], sem="ld")
    S.dma("sync", s2c[:, :], f_s2[:, :], sem="ld")
    S.dma("sync", h1sc[:, :], f_h1s[:, :], sem="ld")
    S.dma("sync", h2sc[:, :], f_h2s[:, :], sem="ld")
    S.dma("sync", w3sc[:, :], f_w3s[:, :], sem="ld")
    S.dma("sync", bprojc[:, :], aa_bproj[:, :], sem="ld")
    S.dma("sync", gammac[:, :], ab_gamma[:, :], sem="ld")
    tok_cst = S.dma("sync", betac[:, :], ab_beta[:, :], sem="ld")
    S.dma("gpsimd", ones_col[:, :], c_ones_col[:, :], sem="bd")
    S.dma("sync", ones_col_f[:, :], c_ones_col[:, :], sem="ld")
    tok_msk = S.dma("gpsimd", maskW[:, :], c_maskW[:, :], sem="bd")
    tok_tab = S.dma("sync", ext_sb[:, 0:2047], ab_tab.rearrange("a b -> b a"),
                    sem="ld", allow_slow_non_contiguous=True)
    tok_z = dve("memset", ext_sb[:, 2047:4095], 0.0, waits=[tok_tab])
    tok_fill = dve("tensor_scalar", ext_sb[:, 2047:4095], ext_sb[:, 2047:4095],
                   ext_sb[:, 2046:2047], None, ALU.add, waits=[tok_z])
    tok_ext = S.dma("sync", extd[:, :], ext_sb[:, :], waits=[tok_fill], sem="ld")
    setup_tok = (tok_cst, tok_msk, tok_ext)

    def emit_ln(src_dram, dst, dst_w, t_base, n_tok, after_toks=()):
        last_sq = None
        last_out = None
        for g in range(n_tok // 512):
            t0 = t_base + g * 512
            sum_ps = pS[2][0:1, 0:512]
            sumsq_ps = pS[2][0:1, 512:1024]
            mm_sum = mm_sq = None
            for cb in range(CB):
                sl = xsr[cb % 2]
                wt = list(after_toks) if (g == 0 and cb < 2) else []
                if last_out is not None:
                    wt.append(last_out)
                if mm_sum is not None and cb >= 2:
                    wt.append(mm_sum)
                if last_sq is not None:
                    wt.append(last_sq)
                ltok = S.dma("sync", sl[:, :],
                             src_dram[cb * 128:(cb + 1) * 128, t0:t0 + 512],
                             waits=wt, sem=f"xl{cb % 2}")
                sq = tsub[:, :]
                wsq = [ltok]
                if mm_sq is not None:
                    wsq.append(mm_sq)
                sqtok = act(AF.Square, sq, sl[:, :], waits=wsq)
                w0 = ps_waits(2, 0) if cb == 0 else []
                mm_sum = mm(sum_ps, ones_col_f[:, :], sl[:, :], start=(cb == 0),
                            stop=(cb == CB - 1), waits=[ltok] + w0, inc="pe")
                w0 = ps_waits(2, 512) if cb == 0 else []
                mm_sq = mm(sumsq_ps, ones_col_f[:, :], sq, start=(cb == 0),
                           stop=(cb == CB - 1), waits=[sqtok] + w0, inc="pe")
                last_sq = sqtok
            tmean = dve("tensor_scalar", r_mean, sum_ps, 1.0 / C, None, ALU.mult,
                        waits=[mm_sum, mm_sq])
            tmsq = dve("tensor_tensor", r_tmp, r_mean, r_mean, ALU.mult, waits=[tmean])
            tvar = dve("scalar_tensor_tensor", r_rstd, sumsq_ps, 1.0 / C, r_tmp,
                       ALU.mult, ALU.subtract, waits=[tmsq])
            ps_set(2, 0, tvar)
            ps_set(2, 512, tvar)
            tvar = dve("tensor_scalar", r_rstd, r_rstd, LN_EPS, None, ALU.add,
                       waits=[tvar])
            tln = act(AF.Ln, r_tmp, r_rstd, waits=[tvar])
            tno = dve("tensor_scalar", r_tmp, r_tmp, -0.5, None, ALU.mult, waits=[tln])
            texp = act(AF.Exp, r_rstd, r_tmp, waits=[tno])
            meanB = pS[3][:, 0:512]
            rstdB = pS[3][:, 512:1024]
            tbm = mm(meanB, ones_row[:, :], r_mean, start=True, stop=True,
                     waits=[tmsq] + ps_waits(3, 0), inc="pe")
            tbr = mm(rstdB, ones_row[:, :], r_rstd, start=True, stop=True,
                     waits=[texp] + ps_waits(3, 512), inc="pe")
            for cb in range(CB):
                sl = xsr[cb % 2]
                wt = [last_out] if last_out is not None else []
                ltok = S.dma("sync", sl[:, :],
                             src_dram[cb * 128:(cb + 1) * 128, t0:t0 + 512],
                             waits=wt, sem=f"xl{cb % 2}")
                tsu = dve("tensor_tensor", tsub[:, :], sl[:, :], meanB, ALU.subtract,
                          waits=[ltok, tbm, tbr])
                dst_sl = dst[:, cb * dst_w + (t0 - t_base): cb * dst_w + (t0 - t_base) + 512]
                last_out = dve("tensor_tensor", dst_sl, tsub[:, :], rstdB, ALU.mult,
                               waits=[tsu])
            ps_set(3, 0, last_out)
            ps_set(3, 512, last_out)
        return last_out

    def vp_dst(tt, oc):
        base = tt * 1040 + 65 * (oc * 8)
        return bass.AP(Vp.tensor, base, [[16640, 128], [65, 8], [1, 64]])

    def ps3d(i):
        return bass.AP(pS[i].tensor, 0, [[1024, 128], [64, 8], [1, 64]])

    def vp_ones_ap():
        return bass.AP(Vp.tensor, 64, [[16640, 128], [1040, 16], [65, 16]])

    def emit_v(stat_fn, rhs_fn, nib, waits0):
        tmem = dve("memset", vp_ones_ap(), 1.0)
        last_e = None
        for tt in range(16):
            for oc in range(2):
                i = (tt * 2 + oc) % 2
                last = None
                for ib in range(nib):
                    w0 = (list(waits0) + ps_waits(i, 0)) if ib == 0 else []
                    last = mm(pS[i][:, 0:512], stat_fn(ib, tt), rhs_fn(ib, oc),
                              start=(ib == 0), stop=(ib == nib - 1),
                              waits=w0, inc="pe" if ib == nib - 1 else None)
                last_e = dve("tensor_copy", vp_dst(tt, oc), ps3d(i), waits=[last, tmem])
                ps_set(i, 0, last_e)
        return last_e

    wslot_free = [None, None, None]
    asub_free = [None] * 8

    ext_bh = sat("ext_bh", [128, 4095], F32, OFF_LAT)

    def ext_diag(c0, wd):
        return bass.AP(ext_bh.tensor, c0 + 127, [[4094, 128], [1, wd]])

    def emit_attention(kind, src_dram, dst_dram, ln_tok):
        scale = SCALE_A if kind == "a" else SCALE_B
        causal = kind == "a"
        store_toks = []

        if kind == "a":
            wq_l, wk_l = wring[0][:, 0:2048], wring[0][:, 2048:4096]
            wv_l, wq_h = wring[1][:, 0:2048], wring[1][:, 2048:4096]
            wk_h, wv_h = wring[2][:, 0:2048], wring[2][:, 2048:4096]
            wload(wq_l, aa_wq_lat, [wslot_free[0]], sem="wd0")
            twk = wload(wk_l, aa_wk_lat, [wslot_free[0]], sem="wd0")
            twq = twk
            wload(wv_l, aa_wv_lat, [wslot_free[1]], sem="wd1")
            twqh = wload(wq_h, aa_wq_h, [wslot_free[1]], sem="wd1")
            twv = twqh
            wload(wk_h, aa_wk_h, [wslot_free[2]], sem="wd2")
            twvh = wload(wv_h, aa_wv_h, [wslot_free[2]], sem="wd2")
            twkh = twvh
            evac = {}
            for (wi, wtok, dst_lat) in ((wq_l, twq, lat_q), (wk_l, twk, lat_k),
                                        (wv_l, twv, lat_v)):
                for ol in range(2):
                    for g in range(QG):
                        i = (ol * QG + g) % 2
                        last = None
                        for ib in range(CB):
                            w0 = ([wtok, ln_tok] + ps_waits(i, 0)) if ib == 0 else []
                            last = mm(pS[i][:, 0:512],
                                      wi[:, ib * 256 + ol * 128: ib * 256 + ol * 128 + 128],
                                      xn[:, ib * T + g * 512: ib * T + g * 512 + 512],
                                      start=(ib == 0), stop=(ib == CB - 1),
                                      waits=w0, inc="pe" if ib == CB - 1 else None)
                        et = act(AF.Copy, dst_lat[:, ol * T + g * 512: ol * T + g * 512 + 512],
                                 pS[i][:, 0:512], waits=[last])
                        ps_set(i, 0, et)
            lat_done = ("act", S.cur("act")[1])
            for ob in range(CB):
                for g in range(QG):
                    i = (ob * QG + g) % 2
                    last = None
                    for ib in range(2):
                        w0 = ([twkh, lat_done] + ps_waits(i, 0)) if ib == 0 else []
                        last = mm(pS[i][:, 0:512],
                                  wk_h[:, ib * 1024 + ob * 128: ib * 1024 + ob * 128 + 128],
                                  lat_k[:, ib * T + g * 512: ib * T + g * 512 + 512],
                                  start=(ib == 0), stop=(ib == 1),
                                  waits=w0, inc="pe" if ib == 1 else None)
                    et = act(AF.Copy, KT[:, ob * T + g * 512: ob * T + g * 512 + 512],
                             pS[i][:, 0:512], waits=[last])
                    ps_set(i, 0, et)
            vput = emit_v(lambda ib, tt: lat_v[:, ib * T + tt * 128: ib * T + tt * 128 + 128],
                          lambda ib, oc: wv_h[:, ib * 1024 + oc * 512: ib * 1024 + oc * 512 + 512],
                          2, [twvh, lat_done])
            qkv_pe = ("pe", S.cur("pe")[1])
            tp0 = wload(wring[0][:, :], aa_wproj, [qkv_pe], col0=0, ncol=512, sem="wd0")
            tp1 = wload(wring[2][:, :], aa_wproj, [qkv_pe], col0=512, ncol=512, sem="wd2")

            def qt_lhs(ib, ob):
                return wq_h[:, ib * 1024 + ob * 128: ib * 1024 + ob * 128 + 128]
            qt_wt = [twqh]
            qt_nib = 2

            def qt_rhs(ib, g):
                return lat_q[:, ib * T + g * 512: ib * T + g * 512 + 512]
            qt_extra = [lat_done]
        else:
            tk0 = S.dma("gpsimd", wring[0][:, :], ab_wkT[0:128, :], waits=[w for w in [wslot_free[0]] if w], sem="wd0")
            tk1 = S.dma("gpsimd", wring[1][:, :], ab_wkT[128:256, :], waits=[w for w in [wslot_free[1]] if w], sem="wd1")
            for ob in range(CB):
                wsl, wtok = (wring[0], tk0) if ob < 4 else (wring[1], tk1)
                obl = ob % 4
                for g in range(QG):
                    i = (ob * QG + g) % 2
                    last = None
                    for ib in range(CB):
                        w0 = ([wtok, ln_tok] + ps_waits(i, 0)) if ib == 0 else []
                        last = mm(pS[i][:, 0:512],
                                  wsl[:, ib * 512 + obl * 128: ib * 512 + obl * 128 + 128],
                                  xn[:, ib * T + g * 512: ib * T + g * 512 + 512],
                                  start=(ib == 0), stop=(ib == CB - 1),
                                  waits=w0, inc="pe" if ib == CB - 1 else None)
                    et = act(AF.Copy, KT[:, ob * T + g * 512: ob * T + g * 512 + 512],
                             pS[i][:, 0:512], waits=[last])
                    ps_set(i, 0, et)
            kt_pe = ("pe", S.cur("pe")[1])
            tv0 = S.dma("gpsimd", wring[0][:, :], ab_wvT[0:128, :], waits=[kt_pe], sem="wd0")
            tv1 = S.dma("gpsimd", wring[1][:, :], ab_wvT[128:256, :], waits=[kt_pe], sem="wd1")
            vput = emit_v(lambda ib, tt: xn[:, ib * T + tt * 128: ib * T + tt * 128 + 128],
                          lambda ib, oc: (wring[0] if oc == 0 else wring[1])[:, ib * 512: ib * 512 + 512],
                          CB, [tv0, tv1, ln_tok])
            v_pe = ("pe", S.cur("pe")[1])
            tq0 = S.dma("gpsimd", wring[0][:, :], ab_wqT[0:128, :], waits=[v_pe], sem="wd0")
            tq1 = S.dma("gpsimd", wring[1][:, :], ab_wqT[128:256, :], waits=[v_pe], sem="wd1")

            def qt_lhs(ib, ob):
                wsl = wring[0] if ob < 4 else wring[1]
                return wsl[:, ib * 512 + (ob % 4) * 128: ib * 512 + (ob % 4) * 128 + 128]
            qt_wt = [tq0, tq1]
            qt_nib = CB
            qt_rhs = lambda ib, g: xn[:, ib * T + g * 512: ib * T + g * 512 + 512]
            qt_extra = []

        kt_all = ("act", S.cur("act")[1])
        ksq_ps = pS[2][0:1, 0:512]
        mmk = None
        sq_hist = {}
        for j in range(32):
            sl_ = ptr[:, (j % 2) * 512:(j % 2) * 512 + 512]
            wsq = [kt_all]
            if j >= 2:
                wsq.append(sq_hist[j - 2])
            sqt = act(AF.Square, sl_, KT[:, j * 512:(j + 1) * 512], waits=wsq)
            w0 = ps_waits(2, 0) if j == 0 else []
            mmk = mm(ksq_ps, ones_col[:, :], sl_, start=(j == 0), stop=(j == 31),
                     waits=[sqt] + w0, inc="pe")
            sq_hist[j] = mmk
        tks = dve("reduce_sum", r_tmp[0:1, 0:1], ksq_ps, AX.X, waits=[mmk])
        ps_set(2, 0, tks)
        tk2 = dve("tensor_scalar", r_tmp[0:1, 0:1], r_tmp[0:1, 0:1],
                  1.0 / (C * T), None, ALU.mult, waits=[tks])
        tkr = act(AF.Ln, scal[0:1, 0:1], r_tmp[0:1, 0:1], waits=[tk2])

        ball_ready = [None, None]
        ball_free = [None, None]
        eb_free = [None]

        def ball_load(h):
            buf = ballb[h % 2]
            sem = "bl0" if h % 2 == 0 else "bl1"
            w = [tok_ext, ("pe", S.cur("pe")[1])]
            if ball_free[h % 2] is not None:
                w.append(ball_free[h % 2])
            if eb_free[0] is not None:
                w.append(eb_free[0])
            src_b = bass.AP(extd.tensor, h * 4095, [[4095, 1], [0, 128], [1, 4095]])
            btok = S.dma("gpsimd", ext_bh[:, :], src_b, waits=w, sem=sem)
            tok = None
            for ci in range(8):
                c0 = ci * 512
                wd_ = 512 if ci < 7 else 384
                tok = S.dma("gpsimd", buf[:, c0:c0 + wd_], ext_diag(c0, wd_),
                            waits=[btok] if ci == 0 else (), sem=sem)
            eb_free[0] = tok
            ball_ready[h % 2] = tok

        prev_qg_dve = None
        prev_recb = None
        prev_recrow = None
        prev_store = None
        proj_toks = None
        slot_teo = [None, None]
        pt_free_carry = [None, None]
        for g in range(QG):
            qts_toks = []
            for ob in range(CB):
                i = ob % 2
                last = None
                for ib in range(qt_nib):
                    w0 = (qt_wt + qt_extra + ps_waits(i, 0) +
                          ([ln_tok] if kind == "b" else [])) if ib == 0 else []
                    last = mm(pS[i][:, 0:512], qt_lhs(ib, ob), qt_rhs(ib, g),
                              start=(ib == 0), stop=(ib == qt_nib - 1),
                              waits=w0, inc="pe" if ib == qt_nib - 1 else None)
                et = act(AF.Copy, qts[:, ob * 512: ob * 512 + 512], pS[i][:, 0:512],
                         waits=[last])
                ps_set(i, 0, et)
                qts_toks.append(et)
            qts_done = qts_toks[-1]

            qsq_ps = pS[2][0:1, 0:512]
            mmq = None
            sq2 = {}
            for j in range(CB):
                sl_ = ptr[:, (j % 2) * 512:(j % 2) * 512 + 512]
                wsq = [qts_done]
                if pt_free_carry[j % 2] is not None:
                    wsq.append(pt_free_carry[j % 2])
                if j >= 2:
                    wsq.append(sq2[j - 2])
                sqt = act(AF.Square, sl_, qts[:, j * 512:(j + 1) * 512], waits=wsq)
                w0 = ps_waits(2, 0) if j == 0 else []
                mmq = mm(qsq_ps, ones_col[:, :], sl_, start=(j == 0), stop=(j == CB - 1),
                         waits=[sqt] + w0, inc="pe")
                sq2[j] = mmq
            tqs = dve("reduce_sum", r_tmp[0:1, 0:1], qsq_ps, AX.X, waits=[mmq])
            ps_set(2, 0, tqs)
            tq2 = dve("tensor_scalar", r_tmp[0:1, 0:1], r_tmp[0:1, 0:1],
                      1.0 / (C * 512), None, ALU.mult, waits=[tqs])
            tql = act(AF.Ln, scal[0:1, 1:2], r_tmp[0:1, 0:1], waits=[tq2])
            tpr = dve("tensor_tensor", scal[0:1, 2:3], scal[0:1, 0:1], scal[0:1, 1:2],
                      ALU.add, waits=[tql, tkr])
            th = dve("tensor_scalar", scal[0:1, 2:3], scal[0:1, 2:3], 0.5, None,
                     ALU.mult, waits=[tpr])
            tex = act(AF.Exp, scal[0:1, 2:3], scal[0:1, 2:3], waits=[th])
            tbs = dve("tensor_scalar", scal[0:1, 2:3], scal[0:1, 2:3], -scale * DH, None,
                      ALU.mult, waits=[tex])
            bps = pS[3][:, 512:513]
            tbc = mm(bps, ones_row[:, :], scal[0:1, 2:3], start=True, stop=True,
                     waits=[tbs] + ps_waits(3, 512), inc="pe")
            tbcol = dve("tensor_copy", biascol[:, :], bps, waits=[tbc])
            ps_set(3, 512, tbcol)

            if kind == "b":
                if prev_qg_dve is not None:
                    ball_free[0] = prev_qg_dve
                ball_load(0)
            nkb = 4 * g + 4 if causal else 16
            sT_free = [None, None]
            pt_free = [pt_free_carry[0], pt_free_carry[1], None, None]
            pv_read = [None, None]
            bv_read = [None, None]
            head_dve = []
            for h in range(NH):
                cbh, po, par = h // 2, 64 * (h % 2), h % 2
                if kind == "b" and h + 1 < NH:
                    ball_load(h + 1)

                pv_ps = pS[2][:, par * 512: par * 512 + 1024 - 512]
                bv_ps = pS[3][:, par * 512: par * 512 + 1024 - 512]
                pv_last = bv_last = None
                pairs = list(range(0, nkb, 2))

                def emit_spair(kb0):
                    si = (kb0 // 2) % 2
                    npair = min(kb0 + 2, nkb) - kb0
                    tokmm = None
                    for j in range(npair):
                        kb = kb0 + j
                        w0 = []
                        if j == 0:
                            w0 = [qts_done, tbcol]
                            if sT_free[si] is not None:
                                w0.append(sT_free[si])
                            w0 += ps_waits(si, "full")
                        tokmm = mm(pS[si][:, j * 512: j * 512 + 512],
                                   KT[po:po + 64, cbh * T + kb * 128: cbh * T + kb * 128 + 128],
                                   qts[po:po + 64, cbh * 512: cbh * 512 + 512],
                                   start=True, stop=True, waits=w0, inc="pe")
                    wexp = [tokmm]
                    for jj in range(npair):
                        sl_ = (kb0 + jj) % 4
                        if pt_free[sl_] is not None:
                            wexp.append(pt_free[sl_])
                    p0 = (kb0 % 4) * 512
                    etok = act(AF.Exp, ptr[:, p0:p0 + npair * 512], pS[si][:, 0:npair * 512],
                               waits=wexp, scale=scale, bias=biascol[:, :])
                    sT_free[si] = etok
                    ps_set(si, "full", etok)
                    for jj in range(npair):
                        exp_toks[kb0 + jj] = etok

                def emit_pvpair(kb0):
                    nonlocal pv_last, bv_last
                    npair = min(kb0 + 2, nkb) - kb0
                    for kb in range(kb0, kb0 + npair):
                        p0 = (kb % 4) * 512
                        ptok = exp_toks[kb]
                        if causal and kb >= 4 * g:
                            j = kb - 4 * g
                            ptok = dve("tensor_tensor", ptr[:, p0:p0 + 512],
                                       ptr[:, p0:p0 + 512],
                                       maskW[:, 384 - 128 * j: 896 - 128 * j], ALU.mult,
                                       waits=[ptok, tok_msk])
                        wt = [ptok, vput]
                        if kb == 0:
                            wt += ps_waits(2, par * 512)
                        pv_last = mm(pv_ps[0:65, :],
                                     Vp[:, kb * 1040 + 65 * h: kb * 1040 + 65 * h + 65],
                                     ptr[:, p0:p0 + 512], start=(kb == 0),
                                     stop=(kb == nkb - 1), waits=wt, inc="pe")
                        pt_free[kb % 4] = pv_last

                exp_toks = {}
                prev_pairk = None
                for kb0 in pairs:
                    emit_spair(kb0)
                    if prev_pairk is not None:
                        emit_pvpair(prev_pairk)
                    prev_pairk = kb0
                emit_pvpair(prev_pairk)
                if kind == "b":
                    for tb in range(16):
                        z0 = 512 * g - 128 * tb + 1920
                        wtb = [ball_ready[h % 2], vput]
                        if tb == 0:
                            wtb += ps_waits(3, par * 512)
                        bv_last = mm(bv_ps[0:64, :],
                                     Vp[:, tb * 1040 + 65 * h: tb * 1040 + 65 * h + 64],
                                     ballb[h % 2][:, z0:z0 + 512],
                                     start=(tb == 0), stop=(tb == 15),
                                     waits=wtb, inc="pe")
                if kind == "b":
                    ball_free[h % 2] = bv_last
                wrc = [pv_last]
                if prev_recb is not None:
                    wrc.append(prev_recb)
                trc = dve("reciprocal", r_recip, pv_ps[64:65, :], waits=wrc)
                wbc = [trc]
                if prev_recrow is not None:
                    wbc.append(prev_recrow)
                tbc2 = S.dma("gpsimd", recipB[:, :],
                             bass.AP(rows.tensor, 0, [[1536, 1], [0, 64], [1, 512]]),
                             waits=wbc, sem="rb")
                prev_recrow = None
                if kind == "a":
                    tep = dve("tensor_tensor", ao_bf[po:po + 64, cbh * 512: cbh * 512 + 512],
                              pv_ps[0:64, :], recipB[:, :], ALU.mult, waits=[tbc2])
                    ps_set(2, par * 512, tep)
                    prev_recb = tep
                else:
                    tmu = dve("tensor_tensor", tsub[0:64, :], pv_ps[0:64, :], recipB[:, :],
                              ALU.mult, waits=[tbc2])
                    ps_set(2, par * 512, tmu)
                    prev_recb = tmu
                    if DBG_NO_BIAS:
                        tep = dve("tensor_copy", ao_f[po:po + 64, cbh * 512: cbh * 512 + 512],
                                  tsub[0:64, :], waits=[tmu, bv_last])
                    else:
                        tep = dve("tensor_tensor", ao_f[po:po + 64, cbh * 512: cbh * 512 + 512],
                                  tsub[0:64, :], bv_ps[0:64, :], ALU.add, waits=[tmu, bv_last])
                    ps_set(3, par * 512, tep)
                prev_recrow = tbc2
                head_dve.append(tep)
            heads_done = head_dve[-1]

            if kind == "b" and DBG_RAW:
                for cb in range(CB):
                    stok = S.dma("sync",
                                 dst_dram[cb * 128:(cb + 1) * 128, g * 512:g * 512 + 512],
                                 ao_f[:, cb * 512: cb * 512 + 512],
                                 waits=[heads_done], sem="st")
                    store_toks.append(stok)
                prev_qg_dve = ("dve", S.cur("dve")[1])
                continue
            if kind == "a":
                if proj_toks is None:
                    proj_toks = (tp0, tp1)
                for ob in range(CB):
                    i = ob % 2
                    wsl = wring[0] if ob < 4 else wring[2]
                    last = None
                    for ib in range(CB):
                        w0 = ([proj_toks[0], proj_toks[1], heads_done] +
                              ps_waits(i, 0)) if ib == 0 else []
                        last = mm(pS[i][:, 0:512],
                                  wsl[:, ib * 512 + (ob % 4) * 128: ib * 512 + (ob % 4) * 128 + 128],
                                  ao_bf[:, ib * 512: ib * 512 + 512],
                                  start=(ib == 0), stop=(ib == CB - 1),
                                  waits=w0, inc="pe" if ib == CB - 1 else None)
                    sl = xsr[ob % 2]
                    wlt = [last]
                    if slot_teo[ob % 2] is not None:
                        wlt.append(slot_teo[ob % 2])
                    ltok = S.dma("sync", sl[:, :],
                                 src_dram[ob * 128:(ob + 1) * 128, g * 512:g * 512 + 512],
                                 waits=wlt, sem=f"xl{ob % 2}")
                    weo = [last, ltok, tok_cst]
                    if prev_store is not None:
                        weo.append(prev_store)
                    teo = dve("scalar_tensor_tensor", t2sb[:, :], pS[i][:, 0:512],
                              bprojc[:, ob:ob + 1], sl[:, :], ALU.add, ALU.add, waits=weo)
                    slot_teo[ob % 2] = teo
                    ps_set(i, 0, teo)
                    prev_store = S.dma("sync",
                                       dst_dram[ob * 128:(ob + 1) * 128, g * 512:g * 512 + 512],
                                       t2sb[:, :], waits=[teo], sem="st")
                    store_toks.append(prev_store)
            else:
                sum_ps = pS[2][0:1, 0:512]
                sumsq_ps = pS[2][0:1, 512:1024]
                mm_sum = mm_sq = None
                for cb in range(CB):
                    sq = tsub[:, :]
                    wsq = [heads_done]
                    if mm_sq is not None:
                        wsq.append(mm_sq)
                    last_sq2 = act(AF.Square, sq, ao_f[:, cb * 512: cb * 512 + 512],
                                   waits=wsq)
                    w0 = ps_waits(2, 0) if cb == 0 else []
                    mm_sum = mm(sum_ps, ones_col_f[:, :], ao_f[:, cb * 512: cb * 512 + 512],
                                start=(cb == 0), stop=(cb == CB - 1),
                                waits=[heads_done] + w0, inc="pe")
                    w0 = ps_waits(2, 512) if cb == 0 else []
                    mm_sq = mm(sumsq_ps, ones_col_f[:, :], sq, start=(cb == 0),
                               stop=(cb == CB - 1), waits=[last_sq2] + w0, inc="pe")
                tmean = dve("tensor_scalar", r_mean, sum_ps, 1.0 / C, None, ALU.mult,
                            waits=[mm_sum, mm_sq])
                tmsq = dve("tensor_tensor", r_tmp, r_mean, r_mean, ALU.mult, waits=[tmean])
                tvar = dve("scalar_tensor_tensor", r_rstd, sumsq_ps, 1.0 / C, r_tmp,
                           ALU.mult, ALU.subtract, waits=[tmsq])
                ps_set(2, 0, tvar)
                ps_set(2, 512, tvar)
                tvar = dve("tensor_scalar", r_rstd, r_rstd, LN_EPS, None, ALU.add,
                           waits=[tvar])
                tln = act(AF.Ln, r_tmp, r_rstd, waits=[tvar])
                tno = dve("tensor_scalar", r_tmp, r_tmp, -0.5, None, ALU.mult, waits=[tln])
                texp = act(AF.Exp, r_rstd, r_tmp, waits=[tno])
                meanB = pS[3][:, 0:512]
                rstdB = pS[3][:, 512:1024]
                tbm = mm(meanB, ones_row[:, :], r_mean, start=True, stop=True,
                         waits=[tmsq] + ps_waits(3, 0), inc="pe")
                tbr = mm(rstdB, ones_row[:, :], r_rstd, start=True, stop=True,
                         waits=[texp] + ps_waits(3, 512), inc="pe")
                lastt = None
                for cb in range(CB):
                    sl = xsr[cb % 2]
                    ltok = S.dma("sync", sl[:, :],
                                 src_dram[cb * 128:(cb + 1) * 128, g * 512:g * 512 + 512],
                                 waits=[lastt] if lastt else [], sem=f"xl{cb % 2}")
                    tsu = dve("tensor_tensor", tsub[:, :], ao_f[:, cb * 512: cb * 512 + 512],
                              meanB, ALU.subtract, waits=[tbm, tbr])
                    tmu2 = dve("tensor_tensor", tsub[:, :], tsub[:, :], rstdB, ALU.mult,
                               waits=[tsu])
                    tgb = dve("tensor_scalar", tsub[:, :], tsub[:, :], gammac[:, cb:cb + 1],
                              betac[:, cb:cb + 1], ALU.mult, op1=ALU.add,
                              waits=[tmu2, tok_cst])
                    weo = [tgb, ltok]
                    if prev_store is not None:
                        weo.append(prev_store)
                    teo = dve("tensor_tensor", t2sb[:, :], tsub[:, :], sl[:, :], ALU.add,
                              waits=weo)
                    lastt = teo
                    prev_store = S.dma("sync",
                                       dst_dram[cb * 128:(cb + 1) * 128, g * 512:g * 512 + 512],
                                       t2sb[:, :], waits=[teo], sem="st")
                    store_toks.append(prev_store)
                ps_set(3, 0, lastt)
                ps_set(3, 512, lastt)
            prev_qg_dve = ("dve", S.cur("dve")[1])
            pt_free_carry[0] = pt_free[0]
            pt_free_carry[1] = pt_free[1]
        pe_end = ("pe", S.cur("pe")[1])
        for _sl in range(3):
            wslot_free[_sl] = pe_end
        return store_toks

    def emit_ffn(src_dram, dst_dram, pre_tok):
        store_toks = []
        prev_store = None
        slot_teo2 = [None, None]
        for ch in range(2):
            tb0 = ch * 1024
            ln_tok = emit_ln(src_dram, xnc, 1024, tb0, 1024,
                             after_toks=[pre_tok] if pre_tok else [])
            slab = {}

            def loadA(ob):
                for ti, wsrc in ((0, f_w1T), (1, f_w2T)):
                    k = (2 * ob + ti) % 8
                    sub = wring[k // 4][:, (k % 4) * 1024:(k % 4) * 1024 + 1024]
                    w = []
                    if asub_free[k] is not None:
                        w.append(asub_free[k])
                    if wslot_free[k // 4] is not None:
                        w.append(wslot_free[k // 4])
                    tok = S.dma("gpsimd", sub, wsrc[ob * 128:(ob + 1) * 128, :],
                                waits=w, sem=f"wa{k}")
                    slab[(ti, ob)] = (tok, k)
            loadA(0)
            loadA(1)
            loadA(2)
            silu_free = None
            aa_last = None
            prev_pair = [None, None]
            for ob in range(IBH):
                if ob + 3 < IBH:
                    loadA(ob + 3)
                par = ob % 2
                u1, u2 = pS[2 * par], pS[2 * par + 1]
                t1, k1 = slab[(0, ob)]
                t2, k2 = slab[(1, ob)]
                last1 = last2 = None
                for (u, tke, kk, which) in ((u1, t1, k1, 1), (u2, t2, k2, 2)):
                    sub = wring[kk // 4][:, (kk % 4) * 1024:(kk % 4) * 1024 + 1024]
                    last = None
                    for ib in range(CB):
                        for hf in range(2):
                            w0 = []
                            if ib == 0 and hf == 0:
                                w0 = [tke, ln_tok] + ps_waits(2 * par + (which - 1), "full")
                                if prev_pair[par] is not None:
                                    w0.append(prev_pair[par])
                            last = mm(u[:, hf * 512: hf * 512 + 512],
                                      sub[:, ib * 128: ib * 128 + 128],
                                      xnc[:, ib * 1024 + hf * 512: ib * 1024 + hf * 512 + 512],
                                      start=(ib == 0), stop=(ib == CB - 1),
                                      waits=w0,
                                      inc="pe" if (ib == CB - 1 and hf == 1) else None)
                    if which == 1:
                        last1 = last
                    else:
                        last2 = last
                wsl_ = [last1]
                if silu_free is not None:
                    wsl_.append(silu_free)
                tsi = act(AF.Silu, silu_sb[:, :], u1[:, :], waits=wsl_,
                          scale=s1c[:, ob:ob + 1])
                taa = dve("scalar_tensor_tensor", AAt[:, ob * 1024: ob * 1024 + 1024],
                          u2[:, :], s2c[:, ob:ob + 1], silu_sb[:, :],
                          ALU.mult, ALU.mult, waits=[tsi, last2])
                silu_free = taa
                aa_last = taa
                prev_pair[par] = taa
                ps_set(2 * par, "full", taa)
                ps_set(2 * par + 1, "full", taa)
                petok = ("pe", S.cur("pe")[1])
                asub_free[k1] = petok
                asub_free[k2] = petok
            petok = ("pe", S.cur("pe")[1])
            wslot_free[0] = petok
            wslot_free[1] = petok

            def hidden(wsrc, scol, src_t, src_tok, dst_t):
                prevp = [None, None]
                out_last = None
                for ob in range(IBH):
                    sl = (ob + 2) % 3
                    w = [wslot_free[sl]] if wslot_free[sl] else []
                    wtok = S.dma("gpsimd", wring[sl][:, :],
                                 wsrc[ob * 128:(ob + 1) * 128, :], waits=w, sem=f"wd{sl}")
                    par = ob % 2
                    ps = pS[2 * par]
                    last = None
                    for ib in range(IBH):
                        for hf in range(2):
                            w0 = []
                            if ib == 0 and hf == 0:
                                w0 = [wtok, src_tok] + ps_waits(2 * par, "full")
                                if prevp[par] is not None:
                                    w0.append(prevp[par])
                            last = mm(ps[:, hf * 512: hf * 512 + 512],
                                      wring[sl][:, ib * 128: ib * 128 + 128],
                                      src_t[:, ib * 1024 + hf * 512: ib * 1024 + hf * 512 + 512],
                                      start=(ib == 0), stop=(ib == IBH - 1),
                                      waits=w0,
                                      inc="pe" if (ib == IBH - 1 and hf == 1) else None)
                    wslot_free[sl] = last
                    tgo = act(AF.Gelu, dst_t[:, ob * 1024: ob * 1024 + 1024], ps[:, :],
                              waits=[last], scale=scol[:, ob:ob + 1])
                    prevp[par] = tgo
                    ps_set(2 * par, "full", tgo)
                    out_last = tgo
                return out_last
            g1_tok = hidden(f_h1T, h1sc, AAt, aa_last, GG)
            g2_tok = hidden(f_h2T, h2sc, GG, g1_tok, AAt)

            prevp = [None, None]
            for ob in range(CB):
                sl = (ob + 2) % 3
                w = [wslot_free[sl]] if wslot_free[sl] else []
                wtok = S.dma("gpsimd", wring[sl][:, :],
                             f_w3T[ob * 128:(ob + 1) * 128, :], waits=w, sem=f"wd{sl}")
                par = ob % 2
                ps = pS[2 * par]
                last = None
                for ib in range(IBH):
                    for hf in range(2):
                        w0 = []
                        if ib == 0 and hf == 0:
                            w0 = [wtok, g2_tok] + ps_waits(2 * par, "full")
                            if prevp[par] is not None:
                                w0.append(prevp[par])
                        last = mm(ps[:, hf * 512: hf * 512 + 512],
                                  wring[sl][:, ib * 128: ib * 128 + 128],
                                  AAt[:, ib * 1024 + hf * 512: ib * 1024 + hf * 512 + 512],
                                  start=(ib == 0), stop=(ib == IBH - 1),
                                  waits=w0,
                                  inc="pe" if (ib == IBH - 1 and hf == 1) else None)
                wslot_free[sl] = last
                teo_last = None
                for hf in range(2):
                    sl2 = xsr[(ob * 2 + hf) % 2]
                    t0 = tb0 + hf * 512
                    wlt = [last]
                    if slot_teo2[(ob * 2 + hf) % 2] is not None:
                        wlt.append(slot_teo2[(ob * 2 + hf) % 2])
                    ltok = S.dma("sync", sl2[:, :],
                                 src_dram[ob * 128:(ob + 1) * 128, t0:t0 + 512],
                                 waits=wlt, sem=f"xl{(ob * 2 + hf) % 2}")
                    weo = [last, ltok]
                    if prev_store is not None:
                        weo.append(prev_store)
                    teo = dve("scalar_tensor_tensor", t2sb[:, :],
                              ps[:, hf * 512: hf * 512 + 512],
                              w3sc[:, ob:ob + 1], sl2[:, :], ALU.mult, ALU.add, waits=weo)
                    slot_teo2[(ob * 2 + hf) % 2] = teo
                    teo_last = teo
                    prev_store = S.dma("sync",
                                       dst_dram[ob * 128:(ob + 1) * 128, t0:t0 + 512],
                                       t2sb[:, :], waits=[teo], sem="st")
                    store_toks.append(prev_store)
                prevp[par] = teo_last
                ps_set(2 * par, "full", teo_last)
        return store_toks

    kindseq = KINDSEQ
    last_tok = None
    for i in range(n_sub):
        src = x_in if i == 0 else xres
        dst = y_out if i == n_sub - 1 else xres
        if kindseq[i] in ("attn_a", "attn_b"):
            after = [t for t in ([last_tok] if last_tok else []) + list(setup_tok)]
            lt = emit_ln(src, xn, T, 0, T, after_toks=after)
            stoks = emit_attention("a" if kindseq[i] == "attn_a" else "b", src, dst, lt)
        else:
            stoks = emit_ffn(src, dst, last_tok)
        last_tok = stoks[-1]

    S.add("sync", None, waits=[S.cur("st")])

    with nc.Block() as block:
        S.emit(block, sems)
    stack.close()
    return nc


_NC_CACHE = None


def kernel(**inputs):
    global _NC_CACHE

    def T_(a):
        return np.ascontiguousarray(np.asarray(a, np.float32).T)

    def pcol(v, nb):
        return np.ascontiguousarray(np.asarray(v, np.float32).reshape(nb, 128).T)

    def perm(w, nob, ow, nib):
        w = np.asarray(w, np.float32).reshape(nob, ow, nib, 128)
        return np.ascontiguousarray(w.transpose(0, 3, 2, 1).reshape(nob * 128, nib * ow))

    x = np.asarray(inputs["L0_attn_a_x"], np.float32)
    B = x.shape[0]
    common = {
        "aa_wq_lat": np.ascontiguousarray(inputs["L0_attn_a_wq_lat"], dtype=np.float32),
        "aa_wk_lat": np.ascontiguousarray(inputs["L0_attn_a_wk_lat"], dtype=np.float32),
        "aa_wv_lat": np.ascontiguousarray(inputs["L0_attn_a_wv_lat"], dtype=np.float32),
        "aa_wq_h": np.ascontiguousarray(inputs["L0_attn_a_wq_heads"], dtype=np.float32),
        "aa_wk_h": np.ascontiguousarray(inputs["L0_attn_a_wk_heads"], dtype=np.float32),
        "aa_wv_h": np.ascontiguousarray(inputs["L0_attn_a_wv_heads"], dtype=np.float32),
        "aa_wproj": np.ascontiguousarray(inputs["L0_attn_a_w_proj"], dtype=np.float32),
        "aa_bproj": pcol(inputs["L0_attn_a_b_proj"], CB),
        "ab_wqT": perm(inputs["L0_attn_b_Wq"], 2, 512, 8),
        "ab_wkT": perm(inputs["L0_attn_b_Wk"], 2, 512, 8),
        "ab_wvT": perm(inputs["L0_attn_b_Wv"], 2, 512, 8),
        "ab_tab": np.ascontiguousarray(inputs["L0_attn_b_bias_table"], dtype=np.float32),
        "ab_gamma": pcol(inputs["L0_attn_b_ln_gamma"], CB),
        "ab_beta": pcol(inputs["L0_attn_b_ln_beta"], CB),
        "f_w1T": perm(inputs["L0_ffn_w1_w"], 32, 128, 8),
        "f_w2T": perm(inputs["L0_ffn_w2_w"], 32, 128, 8),
        "f_h1T": perm(inputs["L0_ffn_h1_w"], 32, 128, 32),
        "f_h2T": perm(inputs["L0_ffn_h2_w"], 32, 128, 32),
        "f_w3T": perm(inputs["L0_ffn_w3_w"], 8, 128, 32),
        "f_s1": pcol(inputs["L0_ffn_w1_s"], IBH),
        "f_s2": pcol(inputs["L0_ffn_w2_s"], IBH),
        "f_h1s": pcol(inputs["L0_ffn_h1_s"], IBH),
        "f_h2s": pcol(inputs["L0_ffn_h2_s"], IBH),
        "f_w3s": pcol(inputs["L0_ffn_w3_s"], CB),
        "c_maskW": (np.arange(896)[None, :] >= np.arange(128)[:, None] + 384
                    ).astype(np.float32),
        "c_ones_row": np.ones((1, 128), np.float32),
        "c_ones_col": np.ones((128, 1), np.float32),
    }
    in_maps = []
    for b in range(B):
        m = dict(common)
        m["x"] = T_(x[b])
        in_maps.append(m)

    if _NC_CACHE is None:
        _NC_CACHE = build_nc()
    res = run_bass_kernel_spmd(_NC_CACHE, in_maps, core_ids=list(range(B)))
    out = np.stack([np.asarray(r["y"], np.float32).T for r in res.results])
    return np.ascontiguousarray(out)


if __name__ == "__main__":
    nc = build_nc()
    ni = sum(len(q) for q in nc.__dict__.get("_dummy", {}) or []) if False else None
    print("built ok")


# version 08
# speedup vs baseline: 1.0322x; 1.0066x over previous
import numpy as np
import ml_dtypes
import concourse.bass as bass
import concourse.mybir as mybir
from concourse.bass_utils import run_bass_kernel_spmd

F32 = mybir.dt.float32
BF16 = mybir.dt.bfloat16
AF = mybir.ActivationFunctionType
ALU = mybir.AluOpType
AX = mybir.AxisListType

T = 2048
C = 1024
HID = 4096
NH = 16
DH = 64
LAT = 256
QG = 4
CB = C // 128
IBH = HID // 128
LN_EPS = 1e-5
SCALE_A = 0.125
SCALE_B = 0.03125
N_SUB = 4
KINDSEQ = ["attn_a", "ffn", "attn_b", "ffn"]
DBG_NO_BIAS = False
DBG_RAW = False

ENGINES = ("sync", "gpsimd", "tensor", "vector", "scalar")

BASE = 16512
OFF_XN = BASE + 0
OFF_QTS = BASE + 32768
OFF_LAT = BASE + 40960
OFF_AA = BASE + 0
OFF_KT = BASE + 65536
OFF_VP = BASE + 98304
OFF_GG = BASE + 65536
OFF_AO = BASE + 131584
OFF_WR = BASE + 147968
OFF_XS = BASE + 172544
OFF_BALL = BASE + 178688
OFF_PT = BASE + 194560
OFF_MASK = BASE + 198656
OFF_RECB = BASE + 200448
OFF_ROWS = BASE + 202496
OFF_CONST = BASE + 208640
SBUF_END = 229376


def ceil32(x):
    return (x + 31) // 32 * 32


class Sched:
    def __init__(self):
        self.q = {e: [] for e in ENGINES}
        self.c = {}

    def cur(self, sem):
        return (sem, self.c.get(sem, 0))

    def add(self, eng, fn, waits=(), inc=None, amt=1):
        tok = None
        for (s, v) in waits:
            assert v <= self.c.get(s, 0), f"wait on future token {s}={v}"
        if inc is not None:
            self.c[inc] = self.c.get(inc, 0) + amt
            tok = (inc, self.c[inc])
        self.q[eng].append((fn, tuple(waits), inc, amt))
        return tok

    def dma(self, eng, out, in_, waits=(), sem=None, **kw):
        def fn(e, out=out, in_=in_, kw=kw):
            return e.dma_start(out=out, in_=in_, **kw)
        return self.add(eng, fn, waits=waits, inc=sem, amt=16)

    def emit(self, block, sems):
        for name in ENGINES:
            ops = self.q[name]

            def make(ops):
                def body(eng):
                    waited = {}
                    for fn, waits, inc, amt in ops:
                        for (s, v) in waits:
                            if v > 0 and waited.get(s, 0) < v:
                                eng.wait_ge(sems[s], v)
                                waited[s] = v
                        if fn is None:
                            continue
                        inst = fn(eng)
                        if inc is not None:
                            inst.then_inc(sems[inc], amt)
                return body
            getattr(block, name)(make(ops))


def build_nc(n_sub=None):
    if n_sub is None:
        n_sub = N_SUB
    nc = bass.Bass("TRN2", target_bir_lowering=False)

    def din(name, shape):
        return nc.dram_tensor(name, shape, F32, kind="ExternalInput").ap()

    x_in = din("x", [C, T])
    dbgd = nc.dram_tensor("dbg", [NH, T], F32, kind="ExternalOutput").ap()
    y_out = nc.dram_tensor("y", [C, T], F32, kind="ExternalOutput").ap()
    xres = nc.dram_tensor("xres", [C, T], F32, kind="Internal").ap()
    extd = nc.dram_tensor("extd", [16, 4095], F32, kind="Internal").ap()

    aa_wq_lat = din("aa_wq_lat", [C, LAT])
    aa_wk_lat = din("aa_wk_lat", [C, LAT])
    aa_wv_lat = din("aa_wv_lat", [C, LAT])
    aa_wq_h = din("aa_wq_h", [LAT, C])
    aa_wk_h = din("aa_wk_h", [LAT, C])
    aa_wv_h = din("aa_wv_h", [LAT, C])
    aa_wproj = din("aa_wproj", [C, C])
    aa_bproj = din("aa_bproj", [128, CB])
    ab_wqT = din("ab_wqT", [2 * 128, 8 * 512])
    ab_wkT = din("ab_wkT", [2 * 128, 8 * 512])
    ab_wvT = din("ab_wvT", [2 * 128, 8 * 512])
    ab_tab = din("ab_tab", [2047, NH])
    ab_gamma = din("ab_gamma", [128, CB])
    ab_beta = din("ab_beta", [128, CB])
    f_w1T = din("f_w1T", [32 * 128, 8 * 128])
    f_w2T = din("f_w2T", [32 * 128, 8 * 128])
    f_h1T = din("f_h1T", [32 * 128, 32 * 128])
    f_h2T = din("f_h2T", [32 * 128, 32 * 128])
    f_w3T = din("f_w3T", [8 * 128, 32 * 128])
    f_s1 = din("f_s1", [128, IBH])
    f_s2 = din("f_s2", [128, IBH])
    f_h1s = din("f_h1s", [128, IBH])
    f_h2s = din("f_h2s", [128, IBH])
    f_w3s = din("f_w3s", [128, CB])
    c_maskW = din("c_maskW", [128, 896])
    c_ones_row = din("c_ones_row", [1, 128])
    c_ones_col = din("c_ones_col", [128, 1])

    def sat(name, shape, dt, off):
        return nc.alloc_sbuf_tensor_at(name, shape, dt, offset=off).ap()

    xn = sat("xn", [128, CB * T], BF16, OFF_XN)
    qts = sat("qts", [128, CB * 512], BF16, OFF_QTS)
    lat_q = sat("lat_q", [128, 2 * T], BF16, OFF_LAT)
    lat_k = sat("lat_k", [128, 2 * T], BF16, OFF_LAT + 8192)
    lat_v = sat("lat_v", [128, 2 * T], BF16, OFF_LAT + 16384)
    AAt = sat("AAt", [128, IBH * 1024], BF16, OFF_AA)
    KT = sat("KT", [128, CB * T], BF16, OFF_KT)
    Vp = sat("Vp", [128, 16 * 1040], BF16, OFF_VP)
    GG = sat("GG", [128, IBH * 1024], BF16, OFF_GG)
    ext_sb = sat("ext_sb", [16, 4095], F32, OFF_AA)
    ao_f = sat("ao_f", [128, CB * 512], F32, OFF_AO)
    ao_bf = sat("ao_bf", [128, CB * 512], BF16, OFF_AO)
    xnc = sat("xnc", [128, CB * 1024], BF16, OFF_AO)
    wring = [sat(f"wr{i}", [128, 4096], BF16, OFF_WR + i * 8192) for i in range(3)]
    xsr = [sat(f"xs{i}", [128, 512], F32, OFF_XS + i * 2048) for i in range(2)]
    ballb = [sat(f"ball{i}", [128, 3968], BF16, OFF_BALL + i * 7936) for i in range(2)]
    ptr = sat("ptr", [128, 4 * 512], BF16, OFF_PT)
    silu_sb = sat("silu_sb", [128, 1024], F32, OFF_PT)
    maskW = sat("maskW", [128, 896], BF16, OFF_MASK)
    recipB = sat("recipB", [64, 512], F32, OFF_RECB)
    rows = sat("rows", [1, 1536], F32, OFF_ROWS)
    co = OFF_CONST
    ones_row = sat("ones_row", [1, 128], F32, co); co = ceil32(co + 512)
    ones_col = sat("ones_col", [128, 1], BF16, co); co = ceil32(co + 4)
    s1c = sat("s1c", [128, IBH], F32, co); co = ceil32(co + IBH * 4)
    s2c = sat("s2c", [128, IBH], F32, co); co = ceil32(co + IBH * 4)
    h1sc = sat("h1sc", [128, IBH], F32, co); co = ceil32(co + IBH * 4)
    h2sc = sat("h2sc", [128, IBH], F32, co); co = ceil32(co + IBH * 4)
    w3sc = sat("w3sc", [128, CB], F32, co); co = ceil32(co + CB * 4)
    bprojc = sat("bprojc", [128, CB], F32, co); co = ceil32(co + CB * 4)
    gammac = sat("gammac", [128, CB], F32, co); co = ceil32(co + CB * 4)
    betac = sat("betac", [128, CB], F32, co); co = ceil32(co + CB * 4)
    biascol = sat("biascol", [128, 1], F32, co); co = ceil32(co + 4)
    kmaxc = sat("kmaxc", [128, 1], F32, co); co = ceil32(co + 4)
    qmaxc = sat("qmaxc", [128, 1], F32, co); co = ceil32(co + 4)
    scr_row = sat("scr_row", [1, 128], F32, co); co = ceil32(co + 512)
    scal = sat("scal", [1, 4], F32, co); co = ceil32(co + 16)
    tsub = sat("tsub", [128, 512], F32, co); co = ceil32(co + 2048)
    t2sb = sat("t2sb", [128, 512], F32, OFF_XS + 4096)
    assert co <= SBUF_END, f"sbuf overflow {co}"

    pS = [nc.alloc_psum_tensor(f"pS{i}", [128, 1024], F32).ap() for i in range(4)]

    semnames = (("ld", "wd", "bd", "st", "pe", "act", "dve", "bl0", "bl1", "rb",
                 "xl0", "xl1") + tuple(f"wd{i}" for i in range(3))
                + tuple(f"wa{i}" for i in range(8)))
    from contextlib import ExitStack
    stack = ExitStack()
    sems = {s: stack.enter_context(nc.semaphore(s)) for s in semnames}

    S = Sched()
    r_mean = rows[0:1, 0:512]
    r_rstd = rows[0:1, 512:1024]
    r_tmp = rows[0:1, 1024:1536]
    r_recip = rows[0:1, 0:512]

    psfree = {}

    def ps_waits(i, part):
        out = []
        if part == "full":
            keys = [(i, 0), (i, 512), (i, "full")]
        else:
            keys = [(i, part), (i, "full")]
        for k in keys:
            if psfree.get(k) is not None:
                out.append(psfree[k])
        return out

    def ps_set(i, part, tok):
        psfree[(i, part)] = tok

    def mm(out, lhsT, rhs, start, stop, waits=(), inc=None):
        def fn(e, out=out, lhsT=lhsT, rhs=rhs, start=start, stop=stop):
            return e.matmul(out, lhsT, rhs, start=start, stop=stop)
        return S.add("tensor", fn, waits=waits, inc=inc)

    def act(fn_type, out, in_, waits=(), scale=1.0, bias=0.0):
        def fn(e, out=out, in_=in_, fn_type=fn_type, scale=scale, bias=bias):
            return e.activation(out, in_, fn_type, bias=bias, scale=scale)
        return S.add("scalar", fn, waits=waits, inc="act")

    def dve(method, *args, waits=(), **kw):
        def fn(e, method=method, args=args, kw=kw):
            return getattr(e, method)(*args, **kw)
        return S.add("vector", fn, waits=waits, inc="dve")

    def wload(dst, src, wait_toks, col0=0, ncol=None, sem="wd"):
        if ncol is None:
            ncol = src.shape[1]
        src_r = src.rearrange("(ib p) o -> p ib o", p=128)[:, :, col0:col0 + ncol]
        return S.dma("gpsimd", dst, src_r, waits=[w for w in wait_toks if w], sem=sem)

    S.dma("sync", ones_row[:, :], c_ones_row[:, :], sem="ld")
    S.dma("sync", s1c[:, :], f_s1[:, :], sem="ld")
    S.dma("sync", s2c[:, :], f_s2[:, :], sem="ld")
    S.dma("sync", h1sc[:, :], f_h1s[:, :], sem="ld")
    S.dma("sync", h2sc[:, :], f_h2s[:, :], sem="ld")
    S.dma("sync", w3sc[:, :], f_w3s[:, :], sem="ld")
    S.dma("sync", bprojc[:, :], aa_bproj[:, :], sem="ld")
    S.dma("sync", gammac[:, :], ab_gamma[:, :], sem="ld")
    tok_cst = S.dma("sync", betac[:, :], ab_beta[:, :], sem="ld")
    S.dma("gpsimd", ones_col[:, :], c_ones_col[:, :], sem="bd")
    tok_msk = S.dma("gpsimd", maskW[:, :], c_maskW[:, :], sem="bd")
    tok_tab = S.dma("sync", ext_sb[:, 0:2047], ab_tab.rearrange("a b -> b a"),
                    sem="ld", allow_slow_non_contiguous=True)
    tok_z = dve("memset", ext_sb[:, 2047:4095], 0.0, waits=[tok_tab])
    tok_fill = dve("tensor_scalar", ext_sb[:, 2047:4095], ext_sb[:, 2047:4095],
                   ext_sb[:, 2046:2047], None, ALU.add, waits=[tok_z])
    tok_ext = S.dma("sync", extd[:, :], ext_sb[:, :], waits=[tok_fill], sem="ld")
    setup_tok = (tok_cst, tok_msk, tok_ext)

    def emit_ln(src_dram, dst, dst_w, t_base, n_tok, after_toks=()):
        last_sq = None
        last_out = None
        for g in range(n_tok // 512):
            t0 = t_base + g * 512
            sum_ps = pS[2][0:1, 0:512]
            sumsq_ps = pS[2][0:1, 512:1024]
            mm_sum = mm_sq = None
            cp_hist = []
            for cb in range(CB):
                sl = xsr[cb % 2]
                wt = list(after_toks) if (g == 0 and cb < 2) else []
                if last_out is not None:
                    wt.append(last_out)
                if cp_hist:
                    wt.append(cp_hist[-1])
                if last_sq is not None:
                    wt.append(last_sq)
                ltok = S.dma("sync", sl[:, :],
                             src_dram[cb * 128:(cb + 1) * 128, t0:t0 + 512],
                             waits=wt, sem=f"xl{cb % 2}")
                xb = ptr[:, (cb % 2) * 512:(cb % 2) * 512 + 512]
                sq = ptr[:, 1024 + (cb % 2) * 512:1024 + (cb % 2) * 512 + 512]
                wcp = [ltok]
                if mm_sum is not None and cb >= 2:
                    wcp.append(mm_sum)
                cptok = dve("tensor_copy", xb, sl[:, :], waits=wcp)
                wsq = [ltok]
                if mm_sq is not None and cb >= 2:
                    wsq.append(mm_sq)
                sqtok = act(AF.Square, sq, sl[:, :], waits=wsq)
                w0 = ps_waits(2, 0) if cb == 0 else []
                mm_sum = mm(sum_ps, ones_col[:, :], xb, start=(cb == 0), stop=(cb == CB - 1),
                            waits=[cptok] + w0, inc="pe")
                w0 = ps_waits(2, 512) if cb == 0 else []
                mm_sq = mm(sumsq_ps, ones_col[:, :], sq, start=(cb == 0), stop=(cb == CB - 1),
                           waits=[sqtok] + w0, inc="pe")
                cp_hist.append(cptok)
                last_sq = sqtok
            tmean = dve("tensor_scalar", r_mean, sum_ps, 1.0 / C, None, ALU.mult,
                        waits=[mm_sum, mm_sq])
            tmsq = dve("tensor_tensor", r_tmp, r_mean, r_mean, ALU.mult, waits=[tmean])
            tvar = dve("scalar_tensor_tensor", r_rstd, sumsq_ps, 1.0 / C, r_tmp,
                       ALU.mult, ALU.subtract, waits=[tmsq])
            ps_set(2, 0, tvar)
            ps_set(2, 512, tvar)
            tvar = dve("tensor_scalar", r_rstd, r_rstd, LN_EPS, None, ALU.add,
                       waits=[tvar])
            tln = act(AF.Ln, r_tmp, r_rstd, waits=[tvar])
            tno = dve("tensor_scalar", r_tmp, r_tmp, -0.5, None, ALU.mult, waits=[tln])
            texp = act(AF.Exp, r_rstd, r_tmp, waits=[tno])
            meanB = pS[3][:, 0:512]
            rstdB = pS[3][:, 512:1024]
            tbm = mm(meanB, ones_row[:, :], r_mean, start=True, stop=True,
                     waits=[tmsq] + ps_waits(3, 0), inc="pe")
            tbr = mm(rstdB, ones_row[:, :], r_rstd, start=True, stop=True,
                     waits=[texp] + ps_waits(3, 512), inc="pe")
            for cb in range(CB):
                sl = xsr[cb % 2]
                wt = [last_out] if last_out is not None else []
                ltok = S.dma("sync", sl[:, :],
                             src_dram[cb * 128:(cb + 1) * 128, t0:t0 + 512],
                             waits=wt, sem=f"xl{cb % 2}")
                tsu = dve("tensor_tensor", tsub[:, :], sl[:, :], meanB, ALU.subtract,
                          waits=[ltok, tbm, tbr])
                dst_sl = dst[:, cb * dst_w + (t0 - t_base): cb * dst_w + (t0 - t_base) + 512]
                last_out = dve("tensor_tensor", dst_sl, tsub[:, :], rstdB, ALU.mult,
                               waits=[tsu])
            ps_set(3, 0, last_out)
            ps_set(3, 512, last_out)
        return last_out

    def vp_dst(tt, oc):
        base = tt * 1040 + 65 * (oc * 8)
        return bass.AP(Vp.tensor, base, [[16640, 128], [65, 8], [1, 64]])

    def ps3d(i):
        return bass.AP(pS[i].tensor, 0, [[1024, 128], [64, 8], [1, 64]])

    def vp_ones_ap():
        return bass.AP(Vp.tensor, 64, [[16640, 128], [1040, 16], [65, 16]])

    def emit_v(stat_fn, rhs_fn, nib, waits0):
        tmem = dve("memset", vp_ones_ap(), 1.0)
        last_e = None
        for tt in range(16):
            for oc in range(2):
                i = (tt * 2 + oc) % 2
                last = None
                for ib in range(nib):
                    w0 = (list(waits0) + ps_waits(i, 0)) if ib == 0 else []
                    last = mm(pS[i][:, 0:512], stat_fn(ib, tt), rhs_fn(ib, oc),
                              start=(ib == 0), stop=(ib == nib - 1),
                              waits=w0, inc="pe" if ib == nib - 1 else None)
                last_e = dve("tensor_copy", vp_dst(tt, oc), ps3d(i), waits=[last, tmem])
                ps_set(i, 0, last_e)
        return last_e

    wslot_free = [None, None, None]
    asub_free = [None] * 8

    ext_bh = sat("ext_bh", [128, 4095], F32, OFF_LAT)

    def ext_diag(c0, wd):
        return bass.AP(ext_bh.tensor, c0 + 127, [[4094, 128], [1, wd]])

    def emit_attention(kind, src_dram, dst_dram, ln_tok):
        scale = SCALE_A if kind == "a" else SCALE_B
        causal = kind == "a"
        store_toks = []

        if kind == "a":
            wq_l, wk_l = wring[0][:, 0:2048], wring[0][:, 2048:4096]
            wv_l, wq_h = wring[1][:, 0:2048], wring[1][:, 2048:4096]
            wk_h, wv_h = wring[2][:, 0:2048], wring[2][:, 2048:4096]
            wload(wq_l, aa_wq_lat, [wslot_free[0]], sem="wd0")
            twk = wload(wk_l, aa_wk_lat, [wslot_free[0]], sem="wd0")
            twq = twk
            wload(wv_l, aa_wv_lat, [wslot_free[1]], sem="wd1")
            twqh = wload(wq_h, aa_wq_h, [wslot_free[1]], sem="wd1")
            twv = twqh
            wload(wk_h, aa_wk_h, [wslot_free[2]], sem="wd2")
            twvh = wload(wv_h, aa_wv_h, [wslot_free[2]], sem="wd2")
            twkh = twvh
            evac = {}
            for (wi, wtok, dst_lat) in ((wq_l, twq, lat_q), (wk_l, twk, lat_k),
                                        (wv_l, twv, lat_v)):
                for ol in range(2):
                    for g in range(QG):
                        i = (ol * QG + g) % 2
                        last = None
                        for ib in range(CB):
                            w0 = ([wtok, ln_tok] + ps_waits(i, 0)) if ib == 0 else []
                            last = mm(pS[i][:, 0:512],
                                      wi[:, ib * 256 + ol * 128: ib * 256 + ol * 128 + 128],
                                      xn[:, ib * T + g * 512: ib * T + g * 512 + 512],
                                      start=(ib == 0), stop=(ib == CB - 1),
                                      waits=w0, inc="pe" if ib == CB - 1 else None)
                        et = act(AF.Copy, dst_lat[:, ol * T + g * 512: ol * T + g * 512 + 512],
                                 pS[i][:, 0:512], waits=[last])
                        ps_set(i, 0, et)
            lat_done = ("act", S.cur("act")[1])
            for ob in range(CB):
                for g in range(QG):
                    i = (ob * QG + g) % 2
                    last = None
                    for ib in range(2):
                        w0 = ([twkh, lat_done] + ps_waits(i, 0)) if ib == 0 else []
                        last = mm(pS[i][:, 0:512],
                                  wk_h[:, ib * 1024 + ob * 128: ib * 1024 + ob * 128 + 128],
                                  lat_k[:, ib * T + g * 512: ib * T + g * 512 + 512],
                                  start=(ib == 0), stop=(ib == 1),
                                  waits=w0, inc="pe" if ib == 1 else None)
                    et = act(AF.Copy, KT[:, ob * T + g * 512: ob * T + g * 512 + 512],
                             pS[i][:, 0:512], waits=[last])
                    ps_set(i, 0, et)
            vput = emit_v(lambda ib, tt: lat_v[:, ib * T + tt * 128: ib * T + tt * 128 + 128],
                          lambda ib, oc: wv_h[:, ib * 1024 + oc * 512: ib * 1024 + oc * 512 + 512],
                          2, [twvh, lat_done])
            qkv_pe = ("pe", S.cur("pe")[1])
            tp0 = wload(wring[0][:, :], aa_wproj, [qkv_pe], col0=0, ncol=512, sem="wd0")
            tp1 = wload(wring[2][:, :], aa_wproj, [qkv_pe], col0=512, ncol=512, sem="wd2")

            def qt_lhs(ib, ob):
                return wq_h[:, ib * 1024 + ob * 128: ib * 1024 + ob * 128 + 128]
            qt_wt = [twqh]
            qt_nib = 2

            def qt_rhs(ib, g):
                return lat_q[:, ib * T + g * 512: ib * T + g * 512 + 512]
            qt_extra = [lat_done]
        else:
            tk0 = S.dma("gpsimd", wring[0][:, :], ab_wkT[0:128, :], waits=[w for w in [wslot_free[0]] if w], sem="wd0")
            tk1 = S.dma("gpsimd", wring[1][:, :], ab_wkT[128:256, :], waits=[w for w in [wslot_free[1]] if w], sem="wd1")
            for ob in range(CB):
                wsl, wtok = (wring[0], tk0) if ob < 4 else (wring[1], tk1)
                obl = ob % 4
                for g in range(QG):
                    i = (ob * QG + g) % 2
                    last = None
                    for ib in range(CB):
                        w0 = ([wtok, ln_tok] + ps_waits(i, 0)) if ib == 0 else []
                        last = mm(pS[i][:, 0:512],
                                  wsl[:, ib * 512 + obl * 128: ib * 512 + obl * 128 + 128],
                                  xn[:, ib * T + g * 512: ib * T + g * 512 + 512],
                                  start=(ib == 0), stop=(ib == CB - 1),
                                  waits=w0, inc="pe" if ib == CB - 1 else None)
                    et = act(AF.Copy, KT[:, ob * T + g * 512: ob * T + g * 512 + 512],
                             pS[i][:, 0:512], waits=[last])
                    ps_set(i, 0, et)
            kt_pe = ("pe", S.cur("pe")[1])
            tv0 = S.dma("gpsimd", wring[0][:, :], ab_wvT[0:128, :], waits=[kt_pe], sem="wd0")
            tv1 = S.dma("gpsimd", wring[1][:, :], ab_wvT[128:256, :], waits=[kt_pe], sem="wd1")
            vput = emit_v(lambda ib, tt: xn[:, ib * T + tt * 128: ib * T + tt * 128 + 128],
                          lambda ib, oc: (wring[0] if oc == 0 else wring[1])[:, ib * 512: ib * 512 + 512],
                          CB, [tv0, tv1, ln_tok])
            v_pe = ("pe", S.cur("pe")[1])
            tq0 = S.dma("gpsimd", wring[0][:, :], ab_wqT[0:128, :], waits=[v_pe], sem="wd0")
            tq1 = S.dma("gpsimd", wring[1][:, :], ab_wqT[128:256, :], waits=[v_pe], sem="wd1")

            def qt_lhs(ib, ob):
                wsl = wring[0] if ob < 4 else wring[1]
                return wsl[:, ib * 512 + (ob % 4) * 128: ib * 512 + (ob % 4) * 128 + 128]
            qt_wt = [tq0, tq1]
            qt_nib = CB
            qt_rhs = lambda ib, g: xn[:, ib * T + g * 512: ib * T + g * 512 + 512]
            qt_extra = []

        kt_all = ("act", S.cur("act")[1])
        ksq_ps = pS[2][0:1, 0:512]
        mmk = None
        sq_hist = {}
        for j in range(32):
            sl_ = ptr[:, (j % 2) * 512:(j % 2) * 512 + 512]
            wsq = [kt_all]
            if j >= 2:
                wsq.append(sq_hist[j - 2])
            sqt = act(AF.Square, sl_, KT[:, j * 512:(j + 1) * 512], waits=wsq)
            w0 = ps_waits(2, 0) if j == 0 else []
            mmk = mm(ksq_ps, ones_col[:, :], sl_, start=(j == 0), stop=(j == 31),
                     waits=[sqt] + w0, inc="pe")
            sq_hist[j] = mmk
        tks = dve("reduce_sum", r_tmp[0:1, 0:1], ksq_ps, AX.X, waits=[mmk])
        ps_set(2, 0, tks)
        tk2 = dve("tensor_scalar", r_tmp[0:1, 0:1], r_tmp[0:1, 0:1],
                  1.0 / (C * T), None, ALU.mult, waits=[tks])
        tkr = act(AF.Ln, scal[0:1, 0:1], r_tmp[0:1, 0:1], waits=[tk2])

        ball_ready = [None, None]
        ball_free = [None, None]
        eb_free = [None]

        def ball_load(h):
            buf = ballb[h % 2]
            sem = "bl0" if h % 2 == 0 else "bl1"
            w = [tok_ext, ("pe", S.cur("pe")[1])]
            if ball_free[h % 2] is not None:
                w.append(ball_free[h % 2])
            if eb_free[0] is not None:
                w.append(eb_free[0])
            src_b = bass.AP(extd.tensor, h * 4095, [[4095, 1], [0, 128], [1, 4095]])
            btok = S.dma("gpsimd", ext_bh[:, :], src_b, waits=w, sem=sem)
            tok = None
            for ci in range(8):
                c0 = ci * 512
                wd_ = 512 if ci < 7 else 384
                tok = S.dma("gpsimd", buf[:, c0:c0 + wd_], ext_diag(c0, wd_),
                            waits=[btok] if ci == 0 else (), sem=sem)
            eb_free[0] = tok
            ball_ready[h % 2] = tok

        prev_qg_dve = None
        prev_recb = None
        prev_recrow = None
        prev_store = None
        proj_toks = None
        slot_teo = [None, None]
        pt_free_carry = [None, None]
        for g in range(QG):
            qts_toks = []
            for ob in range(CB):
                i = ob % 2
                last = None
                for ib in range(qt_nib):
                    w0 = (qt_wt + qt_extra + ps_waits(i, 0) +
                          ([ln_tok] if kind == "b" else [])) if ib == 0 else []
                    last = mm(pS[i][:, 0:512], qt_lhs(ib, ob), qt_rhs(ib, g),
                              start=(ib == 0), stop=(ib == qt_nib - 1),
                              waits=w0, inc="pe" if ib == qt_nib - 1 else None)
                et = act(AF.Copy, qts[:, ob * 512: ob * 512 + 512], pS[i][:, 0:512],
                         waits=[last])
                ps_set(i, 0, et)
                qts_toks.append(et)
            qts_done = qts_toks[-1]

            qsq_ps = pS[2][0:1, 0:512]
            mmq = None
            sq2 = {}
            for j in range(CB):
                sl_ = ptr[:, (j % 2) * 512:(j % 2) * 512 + 512]
                wsq = [qts_done]
                if pt_free_carry[j % 2] is not None:
                    wsq.append(pt_free_carry[j % 2])
                if j >= 2:
                    wsq.append(sq2[j - 2])
                sqt = act(AF.Square, sl_, qts[:, j * 512:(j + 1) * 512], waits=wsq)
                w0 = ps_waits(2, 0) if j == 0 else []
                mmq = mm(qsq_ps, ones_col[:, :], sl_, start=(j == 0), stop=(j == CB - 1),
                         waits=[sqt] + w0, inc="pe")
                sq2[j] = mmq
            tqs = dve("reduce_sum", r_tmp[0:1, 0:1], qsq_ps, AX.X, waits=[mmq])
            ps_set(2, 0, tqs)
            tq2 = dve("tensor_scalar", r_tmp[0:1, 0:1], r_tmp[0:1, 0:1],
                      1.0 / (C * 512), None, ALU.mult, waits=[tqs])
            tql = act(AF.Ln, scal[0:1, 1:2], r_tmp[0:1, 0:1], waits=[tq2])
            tpr = dve("tensor_tensor", scal[0:1, 2:3], scal[0:1, 0:1], scal[0:1, 1:2],
                      ALU.add, waits=[tql, tkr])
            th = dve("tensor_scalar", scal[0:1, 2:3], scal[0:1, 2:3], 0.5, None,
                     ALU.mult, waits=[tpr])
            tex = act(AF.Exp, scal[0:1, 2:3], scal[0:1, 2:3], waits=[th])
            tbs = dve("tensor_scalar", scal[0:1, 2:3], scal[0:1, 2:3], -scale * DH, None,
                      ALU.mult, waits=[tex])
            bps = pS[3][:, 512:513]
            tbc = mm(bps, ones_row[:, :], scal[0:1, 2:3], start=True, stop=True,
                     waits=[tbs] + ps_waits(3, 512), inc="pe")
            tbcol = dve("tensor_copy", biascol[:, :], bps, waits=[tbc])
            ps_set(3, 512, tbcol)

            if kind == "b":
                if prev_qg_dve is not None:
                    ball_free[0] = prev_qg_dve
                ball_load(0)
            nkb = 4 * g + 4 if causal else 16
            sT_free = [None, None]
            pt_free = [pt_free_carry[0], pt_free_carry[1], None, None]
            pv_read = [None, None]
            bv_read = [None, None]
            head_dve = []
            for h in range(NH):
                cbh, po, par = h // 2, 64 * (h % 2), h % 2
                if kind == "b" and h + 1 < NH:
                    ball_load(h + 1)

                pv_ps = pS[2][:, par * 512: par * 512 + 1024 - 512]
                bv_ps = pS[3][:, par * 512: par * 512 + 1024 - 512]
                pv_last = bv_last = None
                pairs = list(range(0, nkb, 2))

                def emit_spair(kb0):
                    si = (kb0 // 2) % 2
                    npair = min(kb0 + 2, nkb) - kb0
                    tokmm = None
                    for j in range(npair):
                        kb = kb0 + j
                        w0 = []
                        if j == 0:
                            w0 = [qts_done, tbcol]
                            if sT_free[si] is not None:
                                w0.append(sT_free[si])
                            w0 += ps_waits(si, "full")
                        tokmm = mm(pS[si][:, j * 512: j * 512 + 512],
                                   KT[po:po + 64, cbh * T + kb * 128: cbh * T + kb * 128 + 128],
                                   qts[po:po + 64, cbh * 512: cbh * 512 + 512],
                                   start=True, stop=True, waits=w0, inc="pe")
                    wexp = [tokmm]
                    for jj in range(npair):
                        sl_ = (kb0 + jj) % 4
                        if pt_free[sl_] is not None:
                            wexp.append(pt_free[sl_])
                    p0 = (kb0 % 4) * 512
                    etok = act(AF.Exp, ptr[:, p0:p0 + npair * 512], pS[si][:, 0:npair * 512],
                               waits=wexp, scale=scale, bias=biascol[:, :])
                    sT_free[si] = etok
                    ps_set(si, "full", etok)
                    for jj in range(npair):
                        exp_toks[kb0 + jj] = etok

                def emit_pvpair(kb0):
                    nonlocal pv_last, bv_last
                    npair = min(kb0 + 2, nkb) - kb0
                    for kb in range(kb0, kb0 + npair):
                        p0 = (kb % 4) * 512
                        ptok = exp_toks[kb]
                        if causal and kb >= 4 * g:
                            j = kb - 4 * g
                            ptok = dve("tensor_tensor", ptr[:, p0:p0 + 512],
                                       ptr[:, p0:p0 + 512],
                                       maskW[:, 384 - 128 * j: 896 - 128 * j], ALU.mult,
                                       waits=[ptok, tok_msk])
                        wt = [ptok, vput]
                        if kb == 0:
                            wt += ps_waits(2, par * 512)
                        pv_last = mm(pv_ps[0:65, :],
                                     Vp[:, kb * 1040 + 65 * h: kb * 1040 + 65 * h + 65],
                                     ptr[:, p0:p0 + 512], start=(kb == 0),
                                     stop=(kb == nkb - 1), waits=wt, inc="pe")
                        pt_free[kb % 4] = pv_last

                exp_toks = {}
                prev_pairk = None
                for kb0 in pairs:
                    emit_spair(kb0)
                    if prev_pairk is not None:
                        emit_pvpair(prev_pairk)
                    prev_pairk = kb0
                emit_pvpair(prev_pairk)
                if kind == "b":
                    for tb in range(16):
                        z0 = 512 * g - 128 * tb + 1920
                        wtb = [ball_ready[h % 2], vput]
                        if tb == 0:
                            wtb += ps_waits(3, par * 512)
                        bv_last = mm(bv_ps[0:64, :],
                                     Vp[:, tb * 1040 + 65 * h: tb * 1040 + 65 * h + 64],
                                     ballb[h % 2][:, z0:z0 + 512],
                                     start=(tb == 0), stop=(tb == 15),
                                     waits=wtb, inc="pe")
                if kind == "b":
                    ball_free[h % 2] = bv_last
                wrc = [pv_last]
                if prev_recb is not None:
                    wrc.append(prev_recb)
                trc = dve("reciprocal", r_recip, pv_ps[64:65, :], waits=wrc)
                wbc = [trc]
                if prev_recrow is not None:
                    wbc.append(prev_recrow)
                tbc2 = S.dma("gpsimd", recipB[:, :],
                             bass.AP(rows.tensor, 0, [[1536, 1], [0, 64], [1, 512]]),
                             waits=wbc, sem="rb")
                prev_recrow = None
                if kind == "a":
                    tep = dve("tensor_tensor", ao_bf[po:po + 64, cbh * 512: cbh * 512 + 512],
                              pv_ps[0:64, :], recipB[:, :], ALU.mult, waits=[tbc2])
                    ps_set(2, par * 512, tep)
                    prev_recb = tep
                else:
                    tmu = dve("tensor_tensor", tsub[0:64, :], pv_ps[0:64, :], recipB[:, :],
                              ALU.mult, waits=[tbc2])
                    ps_set(2, par * 512, tmu)
                    prev_recb = tmu
                    if DBG_NO_BIAS:
                        tep = dve("tensor_copy", ao_f[po:po + 64, cbh * 512: cbh * 512 + 512],
                                  tsub[0:64, :], waits=[tmu, bv_last])
                    else:
                        tep = dve("tensor_tensor", ao_f[po:po + 64, cbh * 512: cbh * 512 + 512],
                                  tsub[0:64, :], bv_ps[0:64, :], ALU.add, waits=[tmu, bv_last])
                    ps_set(3, par * 512, tep)
                prev_recrow = tbc2
                head_dve.append(tep)
            heads_done = head_dve[-1]

            if kind == "b" and DBG_RAW:
                for cb in range(CB):
                    stok = S.dma("sync",
                                 dst_dram[cb * 128:(cb + 1) * 128, g * 512:g * 512 + 512],
                                 ao_f[:, cb * 512: cb * 512 + 512],
                                 waits=[heads_done], sem="st")
                    store_toks.append(stok)
                prev_qg_dve = ("dve", S.cur("dve")[1])
                continue
            if kind == "a":
                if proj_toks is None:
                    proj_toks = (tp0, tp1)
                for ob in range(CB):
                    i = ob % 2
                    wsl = wring[0] if ob < 4 else wring[2]
                    last = None
                    for ib in range(CB):
                        w0 = ([proj_toks[0], proj_toks[1], heads_done] +
                              ps_waits(i, 0)) if ib == 0 else []
                        last = mm(pS[i][:, 0:512],
                                  wsl[:, ib * 512 + (ob % 4) * 128: ib * 512 + (ob % 4) * 128 + 128],
                                  ao_bf[:, ib * 512: ib * 512 + 512],
                                  start=(ib == 0), stop=(ib == CB - 1),
                                  waits=w0, inc="pe" if ib == CB - 1 else None)
                    sl = xsr[ob % 2]
                    wlt = [last]
                    if slot_teo[ob % 2] is not None:
                        wlt.append(slot_teo[ob % 2])
                    ltok = S.dma("sync", sl[:, :],
                                 src_dram[ob * 128:(ob + 1) * 128, g * 512:g * 512 + 512],
                                 waits=wlt, sem=f"xl{ob % 2}")
                    weo = [last, ltok, tok_cst]
                    if prev_store is not None:
                        weo.append(prev_store)
                    teo = dve("scalar_tensor_tensor", t2sb[:, :], pS[i][:, 0:512],
                              bprojc[:, ob:ob + 1], sl[:, :], ALU.add, ALU.add, waits=weo)
                    slot_teo[ob % 2] = teo
                    ps_set(i, 0, teo)
                    prev_store = S.dma("sync",
                                       dst_dram[ob * 128:(ob + 1) * 128, g * 512:g * 512 + 512],
                                       t2sb[:, :], waits=[teo], sem="st")
                    store_toks.append(prev_store)
            else:
                sum_ps = pS[2][0:1, 0:512]
                sumsq_ps = pS[2][0:1, 512:1024]
                mm_sum = mm_sq = None
                last_cp = last_sq2 = None
                for cb in range(CB):
                    xb = ptr[:, (cb % 2) * 512:(cb % 2) * 512 + 512]
                    sq = ptr[:, 1024 + (cb % 2) * 512:1024 + (cb % 2) * 512 + 512]
                    wcp = [heads_done]
                    if mm_sum is not None and cb >= 2:
                        wcp.append(mm_sum)
                    last_cp = dve("tensor_copy", xb, ao_f[:, cb * 512: cb * 512 + 512],
                                  waits=wcp)
                    wsq = [heads_done]
                    if mm_sq is not None and cb >= 2:
                        wsq.append(mm_sq)
                    last_sq2 = act(AF.Square, sq, ao_f[:, cb * 512: cb * 512 + 512],
                                   waits=wsq)
                    w0 = ps_waits(2, 0) if cb == 0 else []
                    mm_sum = mm(sum_ps, ones_col[:, :], xb, start=(cb == 0),
                                stop=(cb == CB - 1), waits=[last_cp] + w0, inc="pe")
                    w0 = ps_waits(2, 512) if cb == 0 else []
                    mm_sq = mm(sumsq_ps, ones_col[:, :], sq, start=(cb == 0),
                               stop=(cb == CB - 1), waits=[last_sq2] + w0, inc="pe")
                tmean = dve("tensor_scalar", r_mean, sum_ps, 1.0 / C, None, ALU.mult,
                            waits=[mm_sum, mm_sq])
                tmsq = dve("tensor_tensor", r_tmp, r_mean, r_mean, ALU.mult, waits=[tmean])
                tvar = dve("scalar_tensor_tensor", r_rstd, sumsq_ps, 1.0 / C, r_tmp,
                           ALU.mult, ALU.subtract, waits=[tmsq])
                ps_set(2, 0, tvar)
                ps_set(2, 512, tvar)
                tvar = dve("tensor_scalar", r_rstd, r_rstd, LN_EPS, None, ALU.add,
                           waits=[tvar])
                tln = act(AF.Ln, r_tmp, r_rstd, waits=[tvar])
                tno = dve("tensor_scalar", r_tmp, r_tmp, -0.5, None, ALU.mult, waits=[tln])
                texp = act(AF.Exp, r_rstd, r_tmp, waits=[tno])
                meanB = pS[3][:, 0:512]
                rstdB = pS[3][:, 512:1024]
                tbm = mm(meanB, ones_row[:, :], r_mean, start=True, stop=True,
                         waits=[tmsq] + ps_waits(3, 0), inc="pe")
                tbr = mm(rstdB, ones_row[:, :], r_rstd, start=True, stop=True,
                         waits=[texp] + ps_waits(3, 512), inc="pe")
                lastt = None
                for cb in range(CB):
                    sl = xsr[cb % 2]
                    ltok = S.dma("sync", sl[:, :],
                                 src_dram[cb * 128:(cb + 1) * 128, g * 512:g * 512 + 512],
                                 waits=[lastt] if lastt else [], sem=f"xl{cb % 2}")
                    tsu = dve("tensor_tensor", tsub[:, :], ao_f[:, cb * 512: cb * 512 + 512],
                              meanB, ALU.subtract, waits=[tbm, tbr])
                    tmu2 = dve("tensor_tensor", tsub[:, :], tsub[:, :], rstdB, ALU.mult,
                               waits=[tsu])
                    tgb = dve("tensor_scalar", tsub[:, :], tsub[:, :], gammac[:, cb:cb + 1],
                              betac[:, cb:cb + 1], ALU.mult, op1=ALU.add,
                              waits=[tmu2, tok_cst])
                    weo = [tgb, ltok]
                    if prev_store is not None:
                        weo.append(prev_store)
                    teo = dve("tensor_tensor", t2sb[:, :], tsub[:, :], sl[:, :], ALU.add,
                              waits=weo)
                    lastt = teo
                    prev_store = S.dma("sync",
                                       dst_dram[cb * 128:(cb + 1) * 128, g * 512:g * 512 + 512],
                                       t2sb[:, :], waits=[teo], sem="st")
                    store_toks.append(prev_store)
                ps_set(3, 0, lastt)
                ps_set(3, 512, lastt)
            prev_qg_dve = ("dve", S.cur("dve")[1])
            pt_free_carry[0] = pt_free[0]
            pt_free_carry[1] = pt_free[1]
        pe_end = ("pe", S.cur("pe")[1])
        for _sl in range(3):
            wslot_free[_sl] = pe_end
        return store_toks

    def emit_ffn(src_dram, dst_dram, pre_tok):
        store_toks = []
        prev_store = None
        slot_teo2 = [None, None]
        for ch in range(2):
            tb0 = ch * 1024
            at = ([pre_tok] if pre_tok else []) if ch == 0 else [store_toks[-1]]
            ln_tok = emit_ln(src_dram, xnc, 1024, tb0, 1024, after_toks=at)
            slab = {}

            def loadA(ob):
                for ti, wsrc in ((0, f_w1T), (1, f_w2T)):
                    k = (2 * ob + ti) % 8
                    sub = wring[k // 4][:, (k % 4) * 1024:(k % 4) * 1024 + 1024]
                    w = []
                    if asub_free[k] is not None:
                        w.append(asub_free[k])
                    if wslot_free[k // 4] is not None:
                        w.append(wslot_free[k // 4])
                    tok = S.dma("gpsimd", sub, wsrc[ob * 128:(ob + 1) * 128, :],
                                waits=w, sem=f"wa{k}")
                    slab[(ti, ob)] = (tok, k)
            loadA(0)
            loadA(1)
            loadA(2)
            silu_free = None
            aa_last = None
            prev_pair = [None, None]
            for ob in range(IBH):
                if ob + 3 < IBH:
                    loadA(ob + 3)
                par = ob % 2
                u1, u2 = pS[2 * par], pS[2 * par + 1]
                t1, k1 = slab[(0, ob)]
                t2, k2 = slab[(1, ob)]
                last1 = last2 = None
                for (u, tke, kk, which) in ((u1, t1, k1, 1), (u2, t2, k2, 2)):
                    sub = wring[kk // 4][:, (kk % 4) * 1024:(kk % 4) * 1024 + 1024]
                    last = None
                    for ib in range(CB):
                        for hf in range(2):
                            w0 = []
                            if ib == 0 and hf == 0:
                                w0 = [tke, ln_tok] + ps_waits(2 * par + (which - 1), "full")
                                if prev_pair[par] is not None:
                                    w0.append(prev_pair[par])
                            last = mm(u[:, hf * 512: hf * 512 + 512],
                                      sub[:, ib * 128: ib * 128 + 128],
                                      xnc[:, ib * 1024 + hf * 512: ib * 1024 + hf * 512 + 512],
                                      start=(ib == 0), stop=(ib == CB - 1),
                                      waits=w0,
                                      inc="pe" if (ib == CB - 1 and hf == 1) else None)
                    if which == 1:
                        last1 = last
                    else:
                        last2 = last
                wsl_ = [last1]
                if silu_free is not None:
                    wsl_.append(silu_free)
                tsi = act(AF.Silu, silu_sb[:, :], u1[:, :], waits=wsl_,
                          scale=s1c[:, ob:ob + 1])
                taa = dve("scalar_tensor_tensor", AAt[:, ob * 1024: ob * 1024 + 1024],
                          u2[:, :], s2c[:, ob:ob + 1], silu_sb[:, :],
                          ALU.mult, ALU.mult, waits=[tsi, last2])
                silu_free = taa
                aa_last = taa
                prev_pair[par] = taa
                ps_set(2 * par, "full", taa)
                ps_set(2 * par + 1, "full", taa)
                petok = ("pe", S.cur("pe")[1])
                asub_free[k1] = petok
                asub_free[k2] = petok
            petok = ("pe", S.cur("pe")[1])
            wslot_free[0] = petok
            wslot_free[1] = petok

            def hidden(wsrc, scol, src_t, src_tok, dst_t):
                prevp = [None, None]
                out_last = None
                for ob in range(IBH):
                    sl = (ob + 2) % 3
                    w = [wslot_free[sl]] if wslot_free[sl] else []
                    wtok = S.dma("gpsimd", wring[sl][:, :],
                                 wsrc[ob * 128:(ob + 1) * 128, :], waits=w, sem=f"wd{sl}")
                    par = ob % 2
                    ps = pS[2 * par]
                    last = None
                    for ib in range(IBH):
                        for hf in range(2):
                            w0 = []
                            if ib == 0 and hf == 0:
                                w0 = [wtok, src_tok] + ps_waits(2 * par, "full")
                                if prevp[par] is not None:
                                    w0.append(prevp[par])
                            last = mm(ps[:, hf * 512: hf * 512 + 512],
                                      wring[sl][:, ib * 128: ib * 128 + 128],
                                      src_t[:, ib * 1024 + hf * 512: ib * 1024 + hf * 512 + 512],
                                      start=(ib == 0), stop=(ib == IBH - 1),
                                      waits=w0,
                                      inc="pe" if (ib == IBH - 1 and hf == 1) else None)
                    wslot_free[sl] = last
                    tgo = act(AF.Gelu, dst_t[:, ob * 1024: ob * 1024 + 1024], ps[:, :],
                              waits=[last], scale=scol[:, ob:ob + 1])
                    prevp[par] = tgo
                    ps_set(2 * par, "full", tgo)
                    out_last = tgo
                return out_last
            g1_tok = hidden(f_h1T, h1sc, AAt, aa_last, GG)
            g2_tok = hidden(f_h2T, h2sc, GG, g1_tok, AAt)

            prevp = [None, None]
            for ob in range(CB):
                sl = (ob + 2) % 3
                w = [wslot_free[sl]] if wslot_free[sl] else []
                wtok = S.dma("gpsimd", wring[sl][:, :],
                             f_w3T[ob * 128:(ob + 1) * 128, :], waits=w, sem=f"wd{sl}")
                par = ob % 2
                ps = pS[2 * par]
                last = None
                for ib in range(IBH):
                    for hf in range(2):
                        w0 = []
                        if ib == 0 and hf == 0:
                            w0 = [wtok, g2_tok] + ps_waits(2 * par, "full")
                            if prevp[par] is not None:
                                w0.append(prevp[par])
                        last = mm(ps[:, hf * 512: hf * 512 + 512],
                                  wring[sl][:, ib * 128: ib * 128 + 128],
                                  AAt[:, ib * 1024 + hf * 512: ib * 1024 + hf * 512 + 512],
                                  start=(ib == 0), stop=(ib == IBH - 1),
                                  waits=w0,
                                  inc="pe" if (ib == IBH - 1 and hf == 1) else None)
                wslot_free[sl] = last
                teo_last = None
                for hf in range(2):
                    sl2 = xsr[(ob * 2 + hf) % 2]
                    t0 = tb0 + hf * 512
                    wlt = [last]
                    if slot_teo2[(ob * 2 + hf) % 2] is not None:
                        wlt.append(slot_teo2[(ob * 2 + hf) % 2])
                    ltok = S.dma("sync", sl2[:, :],
                                 src_dram[ob * 128:(ob + 1) * 128, t0:t0 + 512],
                                 waits=wlt, sem=f"xl{(ob * 2 + hf) % 2}")
                    weo = [last, ltok]
                    if prev_store is not None:
                        weo.append(prev_store)
                    teo = dve("scalar_tensor_tensor", t2sb[:, :],
                              ps[:, hf * 512: hf * 512 + 512],
                              w3sc[:, ob:ob + 1], sl2[:, :], ALU.mult, ALU.add, waits=weo)
                    slot_teo2[(ob * 2 + hf) % 2] = teo
                    teo_last = teo
                    prev_store = S.dma("sync",
                                       dst_dram[ob * 128:(ob + 1) * 128, t0:t0 + 512],
                                       t2sb[:, :], waits=[teo], sem="st")
                    store_toks.append(prev_store)
                prevp[par] = teo_last
                ps_set(2 * par, "full", teo_last)
        return store_toks

    kindseq = KINDSEQ
    last_tok = None
    for i in range(n_sub):
        src = x_in if i == 0 else xres
        dst = y_out if i == n_sub - 1 else xres
        if kindseq[i] in ("attn_a", "attn_b"):
            after = [t for t in ([last_tok] if last_tok else []) + list(setup_tok)]
            lt = emit_ln(src, xn, T, 0, T, after_toks=after)
            stoks = emit_attention("a" if kindseq[i] == "attn_a" else "b", src, dst, lt)
        else:
            stoks = emit_ffn(src, dst, last_tok)
        last_tok = stoks[-1]

    S.add("sync", None, waits=[S.cur("st")])

    with nc.Block() as block:
        S.emit(block, sems)
    stack.close()
    return nc


_NC_CACHE = None


def kernel(**inputs):
    global _NC_CACHE

    def T_(a):
        return np.ascontiguousarray(np.asarray(a, np.float32).T)

    def pcol(v, nb):
        return np.ascontiguousarray(np.asarray(v, np.float32).reshape(nb, 128).T)

    def perm(w, nob, ow, nib):
        w = np.asarray(w, np.float32).reshape(nob, ow, nib, 128)
        return np.ascontiguousarray(w.transpose(0, 3, 2, 1).reshape(nob * 128, nib * ow))

    x = np.asarray(inputs["L0_attn_a_x"], np.float32)
    B = x.shape[0]
    common = {
        "aa_wq_lat": np.ascontiguousarray(inputs["L0_attn_a_wq_lat"], dtype=np.float32),
        "aa_wk_lat": np.ascontiguousarray(inputs["L0_attn_a_wk_lat"], dtype=np.float32),
        "aa_wv_lat": np.ascontiguousarray(inputs["L0_attn_a_wv_lat"], dtype=np.float32),
        "aa_wq_h": np.ascontiguousarray(inputs["L0_attn_a_wq_heads"], dtype=np.float32),
        "aa_wk_h": np.ascontiguousarray(inputs["L0_attn_a_wk_heads"], dtype=np.float32),
        "aa_wv_h": np.ascontiguousarray(inputs["L0_attn_a_wv_heads"], dtype=np.float32),
        "aa_wproj": np.ascontiguousarray(inputs["L0_attn_a_w_proj"], dtype=np.float32),
        "aa_bproj": pcol(inputs["L0_attn_a_b_proj"], CB),
        "ab_wqT": perm(inputs["L0_attn_b_Wq"], 2, 512, 8),
        "ab_wkT": perm(inputs["L0_attn_b_Wk"], 2, 512, 8),
        "ab_wvT": perm(inputs["L0_attn_b_Wv"], 2, 512, 8),
        "ab_tab": np.ascontiguousarray(inputs["L0_attn_b_bias_table"], dtype=np.float32),
        "ab_gamma": pcol(inputs["L0_attn_b_ln_gamma"], CB),
        "ab_beta": pcol(inputs["L0_attn_b_ln_beta"], CB),
        "f_w1T": perm(inputs["L0_ffn_w1_w"], 32, 128, 8),
        "f_w2T": perm(inputs["L0_ffn_w2_w"], 32, 128, 8),
        "f_h1T": perm(inputs["L0_ffn_h1_w"], 32, 128, 32),
        "f_h2T": perm(inputs["L0_ffn_h2_w"], 32, 128, 32),
        "f_w3T": perm(inputs["L0_ffn_w3_w"], 8, 128, 32),
        "f_s1": pcol(inputs["L0_ffn_w1_s"], IBH),
        "f_s2": pcol(inputs["L0_ffn_w2_s"], IBH),
        "f_h1s": pcol(inputs["L0_ffn_h1_s"], IBH),
        "f_h2s": pcol(inputs["L0_ffn_h2_s"], IBH),
        "f_w3s": pcol(inputs["L0_ffn_w3_s"], CB),
        "c_maskW": (np.arange(896)[None, :] >= np.arange(128)[:, None] + 384
                    ).astype(np.float32),
        "c_ones_row": np.ones((1, 128), np.float32),
        "c_ones_col": np.ones((128, 1), np.float32),
    }
    in_maps = []
    for b in range(B):
        m = dict(common)
        m["x"] = T_(x[b])
        in_maps.append(m)

    if _NC_CACHE is None:
        _NC_CACHE = build_nc()
    res = run_bass_kernel_spmd(_NC_CACHE, in_maps, core_ids=list(range(B)))
    out = np.stack([np.asarray(r["y"], np.float32).T for r in res.results])
    return np.ascontiguousarray(out)


if __name__ == "__main__":
    nc = build_nc()
    ni = sum(len(q) for q in nc.__dict__.get("_dummy", {}) or []) if False else None
    print("built ok")
